# version 02
# baseline (speedup 1.0000x reference)
import contextlib

import numpy as np
import ml_dtypes
import concourse.bass as bass
import concourse.mybir as mybir
from concourse.bass_utils import run_bass_kernel_spmd

F32 = mybir.dt.float32
BF16 = mybir.dt.bfloat16
AF = mybir.ActivationFunctionType
ALU = mybir.AluOpType
AX = mybir.AxisListType
bf16 = ml_dtypes.bfloat16

B, S, D = 4, 2048, 2048
NG = 128
QT = float(np.float32(0.5009765625))
MASKVAL = -30000.0
SCALE_Q = float(np.float32(128.0 ** -0.5))
F32EPS = float(np.finfo(np.float32).eps)


class Chan:
    def __init__(self, ctx, nc, name, step=1):
        self.h = ctx.enter_context(nc.semaphore(name))
        self.step = step
        self.n = 0

    def inc(self, inst):
        inst.then_inc(self.h, self.step)
        self.n += 1
        return inst


def _w(eng, chan, n_events):
    if n_events > 0:
        eng.wait_ge(chan.h, n_events * chan.step)


def build_program(taps=(), stop=99):
    nc = bass.Bass("TRN2", target_bir_lowering=False, debug=False)

    xT = nc.dram_tensor("xT", [D, 1024], F32, kind="ExternalInput").ap()
    xresT = nc.dram_tensor("xresT", [D, 1024], F32, kind="ExternalInput").ap()
    wts = {nm: nc.dram_tensor(nm, [16, 128, 2048], F32, kind="ExternalInput").ap()
           for nm in ("wq", "wk", "wv", "wo")}
    wgu_sl = nc.dram_tensor("wgu_sl", [256, 16384], F32, kind="ExternalInput").ap()
    wpr_sl = nc.dram_tensor("wpr_sl", [1024, 2048], F32, kind="ExternalInput").ap()
    out = nc.dram_tensor("out", [D, 1024], F32, kind="ExternalOutput").ap()

    qsl_gu = nc.dram_tensor("qsl_gu", [4096, 1024], BF16)
    gat_gu = nc.dram_tensor("gat_gu", [8 * 4096, 1024], BF16)
    qsl_pr = nc.dram_tensor("qsl_pr", [2048, 1024], BF16)
    gat_pr = nc.dram_tensor("gat_pr", [8 * 2048, 1024], BF16)
    hb_d = nc.dram_tensor("hb_d", [8192, 512], BF16)
    brow_d = nc.dram_tensor("brow_d", [20, 512], F32)

    tap_specs = {"brow0": ([20, 512], F32), "rsb1": ([128, 512], F32),
                 "rows0": ([1, 4096], F32), "X3q0": ([128, 8192], F32),
                 "rms0": ([1, 2048], F32), "rsb0": ([128, 512], F32),
                 "sba0": ([128, 512], F32), "sbb0": ([128, 512], F32),
                 "Qbig": ([128, 16384], BF16), "Kbig": ([128, 16384], BF16),
                 "VOT": ([128, 16384], BF16), "X1": ([128, 16384], F32),
                 "X2b": ([128, 16384], BF16), "hbd": ([8192, 512], BF16),
                 "qgu": ([4096, 1024], BF16), "qpr": ([2048, 1024], BF16),
                 "pq0": ([128, 2048], F32)}
    tap_t = {t: nc.dram_tensor("tap_" + t, *tap_specs[t], kind="ExternalOutput").ap()
             for t in taps}

    p_i = np.arange(128)
    bones_np = np.stack([(p_i < 64), (p_i >= 64)], 1).astype(bf16)
    consts = {
        "ident": np.eye(128, dtype=bf16),
        "mask": np.where((p_i[:, None] % 8) == (p_i[None, :] % 8), 0.0,
                         MASKVAL).astype(np.float32),
        "ones1": np.ones((128, 1), dtype=bf16),
        "bones": bones_np,
        "bcast2": np.ascontiguousarray(bones_np.T),
    }
    cdr = {k: nc.inline_tensor(v, "c_" + k) for k, v in consts.items()}

    ctx = contextlib.ExitStack()
    with ctx:
        def SB(name, shape, dt):
            return ctx.enter_context(nc.sbuf_tensor(name, shape, dt))

        def SB_at(name, shape, dt, base, off_bytes=0):
            addr = nc.lookup_mloc(base).addr + off_bytes
            return nc.alloc_sbuf_tensor_at(name, shape, dt, offset=addr)

        sid = SB("sid", [128, 128], BF16)
        smask = SB("smask", [128, 128], F32)
        sones1 = SB("sones1", [128, 1], BF16)
        sbones = SB("sbones", [128, 2], BF16)
        sbcast2 = SB("sbcast2", [2, 128], BF16)
        sepsA = SB("sepsA", [1, 1], F32)
        sepsB = SB("sepsB", [1, 1], F32)
        rowm = SB("rowm", [1, 512], F32)
        rowv = SB("rowv", [1, 512], F32)
        rowr = SB("rowr", [1, 512], F32)
        rowb = SB("rowb", [1, 512], F32)
        rowt = SB("rowt", [1, 512], F32)
        wbuf = SB("wbuf", [128, 2 * 2048], BF16)
        X2b = SB("X2b", [128, 16 * 1024], BF16)
        xb = SB_at("xb", [128, 16 * 1024], BF16, X2b)

        p_wb = SB("p_wb", [128, 2 * 1024], BF16)
        p_aw = SB("p_aw", [128, 2 * 1024], BF16)
        p_sg = SB("p_sg", [128, 2 * 1024], BF16)
        p_wf = SB("p_wf", [128, 2 * 1024], BF16)
        p_sr = SB("p_sr", [2, 2 * 1024], BF16)
        p_sr2 = SB("p_sr2", [2, 2 * 1024], BF16)

        sg_t = SB_at("sg_t", [128, 2 * 512], F32, p_wb)
        h_t = SB_at("h_t", [128, 2 * 512], F32, p_aw)
        h2b = SB_at("h2b", [128, 4 * 512], BF16, p_sg)
        hbt = SB_at("hbt", [128, 2 * 512], BF16, p_wf)
        rstd_sb = SB_at("rstd_sb", [128, 512], F32, p_sr)

        Qbig = SB("Qbig", [128, 16 * 1024], BF16)
        Kbig = SB("Kbig", [128, 16 * 1024], BF16)
        X1 = SB_at("X1", [128, 16 * 1024], F32, Qbig)
        sgu = SB_at("sgu", [128, 32 * 1024], BF16, Qbig)

        VOT = SB("VOT", [128, 16 * 1024], BF16)
        X3q = SB_at("X3q", [128, 16 * 512], F32, VOT)

        Sm = SB("Sm", [128, 4 * 128], F32)
        Ebuf = SB("Ebuf", [128, 4 * 128], F32)
        Abuf = SB("Abuf", [128, 4 * 128], BF16)
        Ats = SB("Ats", [128, 4 * 128], BF16)
        Vss = SB("Vss", [128, 4 * 128], BF16)
        negmx = SB("negmx", [128, 4], F32)
        sumrow = SB("sumrow", [128, 4], F32)
        rcb = SB("rcb", [128, 4], F32)
        t2 = SB_at("t2", [128, 512], F32, Sm)
        qstg = SB_at("qstg", [128, 2 * 512], BF16, Sm)
        p_cmp = SB_at("p_cmp", [128, 512], BF16, Sm)
        p_qs = SB_at("p_qs", [128, 512], BF16, Ebuf)
        xres = SB("xres", [128, 3 * 1024], F32)
        sb_a = SB("sb_a", [128, 512], F32)
        sb_b = SB("sb_b", [128, 512], F32)
        X1b = SB("X1b", [128, 2 * 512], BF16)
        X1sq = SB("X1sq", [128, 2 * 512], BF16)
        t1 = SB("t1", [128, 2 * 512], F32)
        outt = SB("outt", [128, 2 * 512], F32)
        sprb = SB_at("sprb", [128, 2 * 1024], BF16, X1b)
        X3b = SB_at("X3b", [128, 512], BF16, t1)
        X3s = SB_at("X3s", [128, 512], BF16, t1, 1024)
        hbr = SB_at("hbr", [128, 4 * 512], BF16, t1)

        rem = nc.sbuf_bytes_remaining
        assert rem >= 0, f"SBUF over budget by {-rem} bytes"

        block = ctx.enter_context(nc.Block())
        CH = lambda name, step=1: Chan(ctx, nc, name, step)

        pw = CH("pw", 16); pa = CH("pa"); pm = CH("pm")
        psc = CH("psc"); pvr = CH("pvr"); pb = CH("pb"); pvq = CH("pvq")
        pvb = CH("pvb"); pqw = CH("pqw", 16); cc1 = CH("cc1"); cc2 = CH("cc2")
        gW = CH("gW", 16); tq = CH("tq"); aev = CH("aev"); vev = CH("vev")
        tS = CH("tS"); vme = CH("vme"); vmx = CH("vmx"); aex = CH("aex")
        vA = CH("vA"); tat = CH("tat"); vat = CH("vat"); tvs = CH("tvs")
        avs = CH("avs"); tm2 = CH("tm2"); aot = CH("aot")
        sx = CH("sx", 16); tq3 = CH("tq3"); vx1 = CH("vx1")
        aln = CH("aln"); tst = CH("tst"); rws = CH("rws"); tbc = CH("tbc")
        vap = CH("vap")
        sl = CH("sl", 16); tgu = CH("tgu"); asl = CH("asl"); vh = CH("vh")
        ahb = CH("ahb"); ghb = CH("ghb", 16); shr = CH("shr", 16)
        ah2 = CH("ah2"); tsq = CH("tsq"); arr_ = CH("arr_")
        tpk = CH("tpk"); vx3 = CH("vx3"); a2c = CH("a2c")
        t2s = CH("t2s"); t2b = CH("t2b"); vo = CH("vo"); so = CH("so", 16)
        bcd = CH("bcd", 16)
        cst = CH("cst", 16); gtap = CH("gtap", 16)

        p_items = [("gu", kk, sub) for kk in range(2) for sub in range(16)] + \
                  [("pr", kk, sub) for kk in range(8) for sub in range(2)]
        NPI = len(p_items)

        def p_src(item):
            w_, kk, sub = item
            t = wgu_sl if w_ == "gu" else wpr_sl
            return t[128 * kk:128 * (kk + 1), 1024 * sub:1024 * (sub + 1)]

        def p_dst(item):
            w_, kk, sub = item
            if w_ == "gu":
                return qsl_gu[256 * sub + 128 * kk:256 * sub + 128 * kk + 128, :]
            return qsl_pr[256 * kk + 128 * sub:256 * kk + 128 * sub + 128, :]

        @block.sync
        def _(s):
            for _i in range(8):
                s.nop(cycle_cnt=4000)
            for k, t in cdr.items():
                dst = {"ident": sid, "mask": smask, "ones1": sones1,
                       "bones": sbones, "bcast2": sbcast2}[k]
                cst.inc(s.dma_start(out=dst[:, :], in_=t.ap()))
            for jo in (range(16) if stop >= 3 else []):
                if jo >= 3:
                    _w(s, vx1, 2 * (jo - 2))
                sx.inc(s.dma_start(
                    out=xres[:, 1024 * (jo % 3):1024 * (jo % 3) + 1024],
                    in_=xresT[128 * jo:128 * (jo + 1), :]))

        @block.vector
        def _(v):
            for _i in range(8):
                v.nop(cycle_cnt=4000)
            v.memset(sepsA[:], 1e-5)
            v.memset(sepsB[:], F32EPS)

        @block.gpsimd
        def _(g):
            for _i in range(8):
                g.nop(cycle_cnt=4000)
            for k in range(16):
                gW.inc(g.dma_start(out=xb[:, 1024 * k:1024 * (k + 1)],
                                   in_=xT[128 * k:128 * (k + 1), :]))
            for pi, nm in enumerate(("wq", "wk", "wv")):
                for j in range(16):
                    L = pi * 16 + j
                    if L == 2:
                        _w(g, tq, 6)
                    elif L == 3:
                        _w(g, tq, 7)
                    elif L >= 4:
                        _w(g, tq, 2 * L)
                    gW.inc(g.dma_start(
                        out=wbuf[:, 2048 * (L % 2):2048 * (L % 2) + 2048],
                        in_=wts[nm][j]))
            for j in (range(16) if stop >= 3 else []):
                if j < 2:
                    _w(g, tq, 96 + 2 * j)
                else:
                    _w(g, tq3, 2 * (j - 2) + 2)
                gW.inc(g.dma_start(
                    out=wbuf[:, 2048 * (j % 2):2048 * (j % 2) + 2048],
                    in_=wts["wo"][j]))
            if stop >= 5:
                for i in (0, 1):
                    pw.inc(g.dma_start(
                        out=p_wb[:, 1024 * (i % 2):1024 * (i % 2) + 1024],
                        in_=p_src(p_items[i])))
                for i, item in enumerate(p_items):
                    _w(g, pvq, 2 * (i + 1))
                    pqw.inc(g.dma_start(
                        out=p_dst(item),
                        in_=p_wf[:, 1024 * (i % 2):1024 * (i % 2) + 1024]))
                    if i + 2 < len(p_items):
                        pw.inc(g.dma_start(
                            out=p_wb[:, 1024 * (i % 2):1024 * (i % 2) + 1024],
                            in_=p_src(p_items[i + 2])))
            if stop >= 5:
                _w(g, pqw, 48)
                cc1.inc(g.collective_compute(
                    "AllGather", ALU.bypass, replica_groups=[list(range(8))],
                    ins=[qsl_gu.ap().opt()], outs=[gat_gu.ap().opt()]))
                cc2.inc(g.collective_compute(
                    "AllGather", ALU.bypass, replica_groups=[list(range(8))],
                    ins=[qsl_pr.ap().opt()], outs=[gat_pr.ap().opt()]))
            for th in (range(2 if stop >= 7 else 1) if stop >= 6 else []):
                for u in range(64):
                    _w(g, ahb, 64 * th + u + 1)
                    ghb.inc(g.dma_start(
                        out=hb_d[128 * u:128 * (u + 1), :],
                        in_=hbt[:, 512 * (u % 2):512 * (u % 2) + 512]))
            if "qgu" in tap_t:
                _w(g, cc1, 1)
                gtap.inc(g.dma_start(out=tap_t["qgu"], in_=qsl_gu.ap()))
            if "qpr" in tap_t:
                _w(g, cc2, 1)
                gtap.inc(g.dma_start(out=tap_t["qpr"], in_=qsl_pr.ap()))
            if "hbd" in tap_t:
                _w(g, ghb, 128)
                gtap.inc(g.dma_start(out=tap_t["hbd"], in_=hb_d.ap()))
            if stop >= 7:
                g.wait_ge(so.h, 32 * 16)
            elif stop == 6:
                g.wait_ge(so.h, 16 * 16)
            elif stop == 5:
                g.wait_ge(cc2.h, 1)
                g.wait_ge(pqw.h, 48 * 16)
            elif stop == 4:
                g.wait_ge(vap.h, 32)
            elif stop == 3:
                g.wait_ge(vx1.h, 32)
            elif stop == 2:
                g.wait_ge(aot.h, 128)
            elif stop == 1:
                g.wait_ge(aev.h, 32)
                g.wait_ge(vev.h, 68)
            if taps:
                g.wait_ge(gtap.h, 16 * gtap.n)

        with nc.psum_tensor("pq", [128, 4 * 512], F32) as pq:
            @block.tensor
            def _(t):
                for _i in range(8):
                    t.nop(cycle_cnt=4000)
                _w(t, cst, 5)
                _w(t, gW, 16)
                sched = [(0, 0), (0, 1), (1, 0), (1, 1), (0, 0), (0, 1)]
                sched += [(L, th) for L in range(2, 48) for th in range(2)]
                for L, th in sched:
                    n = 2 * L + th
                    _w(t, gW, 16 + L + 1)
                    m4 = n - 4
                    if m4 >= 0:
                        if 32 <= m4 < 64:
                            _w(t, aev, m4 - 31)
                        elif m4 < 32:
                            _w(t, vev, 1 if m4 == 0 else
                               (3 if m4 == 1 else m4 + 3))
                        else:
                            _w(t, vev, m4 - 29)
                    for k in range(16):
                        inst = t.matmul(
                            pq[:, 512 * (n % 4):512 * (n % 4) + 512],
                            lhsT=wbuf[:, 2048 * (L % 2) + 128 * k:
                                      2048 * (L % 2) + 128 * (k + 1)],
                            rhs=xb[:, 1024 * k + 512 * th:
                                   1024 * k + 512 * th + 512],
                            start=(k == 0), stop=(k == 15))
                    tq.inc(inst)

            @block.scalar
            def _(a):
                for _i in range(8):
                    a.nop(cycle_cnt=4000)
                for j in range(16):
                    for th in range(2):
                        n = 2 * (16 + j) + th
                        _w(a, tq, min(n + 4, 98))
                        src = pq[:, 512 * (n % 4):512 * (n % 4) + 512]
                        aev.inc(a.activation(
                            out=Kbig[:, 1024 * j + 512 * th:
                                     1024 * j + 512 * th + 512],
                            in_=src, func=AF.Copy, scale=SCALE_Q))
                if "Qbig" in tap_t:
                    _w(a, vev, 68)
                    gtap.inc(a.dma_start(out=tap_t["Qbig"], in_=Qbig[:, :]))
                if "Kbig" in tap_t:
                    gtap.inc(a.dma_start(out=tap_t["Kbig"], in_=Kbig[:, :]))

            @block.vector
            def _(v):
                for _i in range(8):
                    v.memset(qstg[:, 512 * (_i % 2):512 * (_i % 2) + 512], 0.0)
                hq = Qbig[:, :].rearrange("p (g t) -> p g t", t=128)
                hv = VOT[:, :].rearrange("p (g t) -> p g t", t=128)
                for pi in (0, 2):
                    for j in range(16):
                        for th in range(2):
                            n = 2 * (pi * 16 + j) + th
                            S = 5 + n if n < 2 else n + 3
                            _w(v, tq, min(S + 1, 98))
                            if n >= 95:
                                v.nop(cycle_cnt=512)
                            src = pq[:, 512 * (n % 4):512 * (n % 4) + 512]
                            dst = (hq if pi == 0 else hv)[
                                :, 64 * th:64 * (th + 1), 8 * j:8 * (j + 1)]
                            if n < 2:
                                vev.inc(v.tensor_copy(
                                    out=qstg[:, 512 * th:512 * th + 512],
                                    in_=src))
                                vev.inc(v.tensor_copy(
                                    out=dst,
                                    in_=qstg[:, 512 * th:512 * th + 512]))
                            else:
                                vev.inc(v.tensor_copy(out=dst, in_=src))
                for th in range(2):
                    vev.inc(v.tensor_copy(
                        out=hq[:, 64 * th:64 * (th + 1), 0:8],
                        in_=qstg[:, 512 * th:512 * th + 512]))
                v.memset(sepsA[:], 1e-5)
                v.memset(sepsB[:], F32EPS)

        Kv3 = Kbig[:, :].rearrange("p (t s) -> p t s", t=16)

        def s2(g):
            return slice(512 * (g % 2), 512 * (g % 2) + 128)

        def s2w(g):
            return slice(1024 * (g % 2), 1024 * (g % 2) + 128)

        def b4(g):
            return slice(128 * (g % 4), 128 * (g % 4) + 128)

        with (
            nc.psum_tensor("pS", [128, 1024], F32) as pS,
            nc.psum_tensor("pat", [128, 2048], BF16) as pat,
            nc.psum_tensor("pvs", [128, 2048], BF16) as pvs,
            nc.psum_tensor("po2", [128, 1024], F32) as po2,
        ):
            @block.tensor
            def _(t):
                if stop < 2:
                    return
                _w(t, aev, 32)
                _w(t, vev, 68)
                for g in range(NG + 2):
                    if g >= 2:
                        x = g - 2
                        _w(t, vA, x + 1)
                        _w(t, vat, x - 1)
                        tat.inc(t.transpose(pat[:, s2w(x)], in_=Abuf[:, b4(x)],
                                            identity=sid[:, :]))
                    if g < NG:
                        _w(t, vme, g - 1)
                        tS.inc(t.matmul(pS[:, s2(g)],
                                        lhsT=Qbig[:, 128 * g:128 * (g + 1)],
                                        rhs=Kv3[:, :, 8 * g:8 * (g + 1)],
                                        start=True, stop=True))
                        _w(t, avs, g - 1)
                        tvs.inc(t.transpose(pvs[:, s2w(g)],
                                            in_=VOT[:, 128 * g:128 * (g + 1)],
                                            identity=sid[:, :]))
                    if g >= 2:
                        x = g - 2
                        _w(t, vat, x + 1)
                        _w(t, avs, x + 1)
                        _w(t, aot, x - 1)
                        tm2.inc(t.matmul(po2[:, s2(x)], lhsT=Vss[:, b4(x)],
                                         rhs=Ats[:, b4(x)], start=True, stop=True))

            @block.vector
            def _(v):
                if stop < 2:
                    return
                for g in range(NG + 2):
                    if g < NG:
                        _w(v, tS, g + 1)
                        _w(v, tvs, g + 1)
                        if g < 4:
                            v.nop(cycle_cnt=256)
                        _w(v, aex, g - 3)
                        vme.inc(v.tensor_tensor(out=Sm[:, b4(g)], in0=pS[:, s2(g)],
                                                in1=smask[:, :], op=ALU.add))
                        vmx.inc(v.reduce_max(out=negmx[:, g % 4:g % 4 + 1],
                                             in_=Sm[:, b4(g)], axis=AX.X,
                                             negate=True))
                        _w(v, aex, g + 1)
                        _w(v, tat, g - 3)
                        vA.inc(v.tensor_scalar_mul(out=Abuf[:, b4(g)],
                                                   in0=Ebuf[:, b4(g)],
                                                   scalar1=rcb[:, g % 4:g % 4 + 1]))
                    if g >= 2:
                        x = g - 2
                        _w(v, tat, x + 1)
                        _w(v, tS, min(x + 3, NG))
                        if x >= NG - 3:
                            v.nop(cycle_cnt=128)
                        _w(v, tm2, x - 3)
                        vat.inc(v.tensor_copy(out=Ats[:, b4(x)], in_=pat[:, s2w(x)]))

            @block.scalar
            def _(a):
                if stop < 2:
                    return
                for g in range(NG + 2):
                    if 1 <= g <= NG:
                        y = g - 1
                        _w(a, tvs, y + 1)
                        if y >= 2:
                            _w(a, tm2, y - 1)
                        else:
                            a.nop(cycle_cnt=512)
                        _w(a, tm2, y - 3)
                        avs.inc(a.activation(out=Vss[:, b4(y)], in_=pvs[:, s2w(y)],
                                             func=AF.Copy))
                    if g < NG:
                        _w(a, vmx, g + 1)
                        _w(a, vA, g - 3)
                        a.activation(out=Ebuf[:, b4(g)], in_=Sm[:, b4(g)],
                                     func=AF.Exp,
                                     bias=negmx[:, g % 4:g % 4 + 1],
                                     accum_out=sumrow[:, g % 4:g % 4 + 1])
                        a.drain(fusable=False)
                        a.activation(out=negmx[:, g % 4:g % 4 + 1],
                                     in_=sumrow[:, g % 4:g % 4 + 1], func=AF.Ln)
                        a.drain(fusable=False)
                        a.activation(out=rcb[:, g % 4:g % 4 + 1],
                                     in_=negmx[:, g % 4:g % 4 + 1],
                                     func=AF.Exp, scale=-1.0)
                        aex.inc(a.drain(fusable=False))
                    if g >= 2:
                        x = g - 2
                        _w(a, tm2, x + 1)
                        _w(a, tat, min(x + 2, NG))
                        if x >= NG - 1:
                            a.nop(cycle_cnt=256)
                        aot.inc(a.activation(out=VOT[:, 128 * x:128 * (x + 1)],
                                             in_=po2[:, s2(x)], func=AF.Copy))
                if "VOT" in tap_t:
                    gtap.inc(a.dma_start(out=tap_t["VOT"], in_=VOT[:, :]))

        with nc.psum_tensor("po3", [128, 4 * 512], F32) as po3:
            @block.tensor
            def _(t):
                if stop < 3:
                    return
                _w(t, vme, NG); _w(t, vat, NG); _w(t, avs, NG); _w(t, aot, NG)
                for jo in range(16):
                    _w(t, gW, 16 + 48 + jo + 1)
                    for th in range(2):
                        n3 = 2 * jo + th
                        _w(t, vx1, n3 - 3)
                        v5 = VOT[:, 8192 * th:8192 * (th + 1)].rearrange(
                            "p (a b h s) -> p a b h s", a=32, b=2, h=16, s=8)
                        for m in range(16):
                            inst = t.matmul(
                                po3[:, 512 * (n3 % 4):512 * (n3 % 4) + 512],
                                lhsT=wbuf[:, 2048 * (jo % 2) + 128 * m:
                                          2048 * (jo % 2) + 128 * (m + 1)],
                                rhs=v5[:, :, m // 8, :, m % 8],
                                start=(m == 0), stop=(m == 15))
                        tq3.inc(inst)

            @block.vector
            def _(v):
                if stop < 3:
                    return
                for jo in range(16):
                    for th in range(2):
                        n3 = 2 * jo + th
                        _w(v, tq3, min(n3 + 2, 32))
                        if n3 >= 31:
                            v.nop(cycle_cnt=256)
                        _w(v, sx, jo + 1)
                        vx1.inc(v.tensor_tensor(
                            out=X1[:, 1024 * jo + 512 * th:1024 * jo + 512 * th + 512],
                            in0=po3[:, 512 * (n3 % 4):512 * (n3 % 4) + 512],
                            in1=xres[:, 1024 * (jo % 3) + 512 * th:
                                     1024 * (jo % 3) + 512 * th + 512],
                            op=ALU.add))

        for th in (range(2) if stop >= 4 else []):
            aln_b, tst_b, rws_b, tbc_b = aln.n, tst.n, rws.n, tbc.n
            with (
                nc.psum_tensor(f"ps1_{th}", [1, 512], F32) as ps1,
                nc.psum_tensor(f"pq1_{th}", [1, 512], F32) as pq1,
            ):
                @block.scalar
                def _(a):
                    if th == 0 and "X1" in tap_t:
                        _w(a, vx1, 32)
                        gtap.inc(a.dma_start(out=tap_t["X1"], in_=X1[:, :]))
                    for k in range(16):
                        if k == 0:
                            _w(a, vx1, 32)
                        _w(a, tst, tst_b + max(0, k - 1))
                        src = X1[:, 1024 * k + 512 * th:1024 * k + 512 * th + 512]
                        a.activation(out=X1b[:, 512 * (k % 2):512 * (k % 2) + 512],
                                     in_=src, func=AF.Copy)
                        aln.inc(a.activation(
                            out=X1sq[:, 512 * (k % 2):512 * (k % 2) + 512],
                            in_=src, func=AF.Square))
                    _w(a, tst, tst_b + 16)
                    a.nop(cycle_cnt=512)
                    a.activation(out=rowm[:, :], in_=ps1[0:1, :], func=AF.Copy,
                                 scale=1.0 / 2048.0)
                    a.activation(out=rowv[:, :], in_=pq1[0:1, :],
                                 func=AF.Copy, scale=1.0 / 2048.0)
                    rws.inc(a.drain(fusable=False))
                    _w(a, rws, rws_b + 2)
                    a.activation(out=rowv[:, :], in_=rowt[:, :], func=AF.Ln,
                                 bias=sepsA[0:1, 0:1])
                    a.drain(fusable=False)
                    a.activation(out=rowr[:, :], in_=rowv[:, :],
                                 func=AF.Exp, scale=-0.5)
                    rws.inc(a.drain(fusable=False))
                    if th == 1:
                        _w(a, vap, 16)
                    r0 = 2 * th
                    bcd.inc(a.dma_start(out=brow_d.ap()[r0:r0 + 1, :],
                                        in_=rowr[0:1, :]))
                    _w(a, bcd, bcd.n)
                    a.nop(cycle_cnt=4000)
                    bcd.inc(a.dma_start(
                        out=sb_a[:, :],
                        in_=brow_d.ap()[r0:r0 + 1, :].broadcast_to((128, 512))))
                    _w(a, rws, rws_b + 4)
                    bcd.inc(a.dma_start(out=brow_d.ap()[r0 + 1:r0 + 2, :],
                                        in_=rowb[0:1, :]))
                    _w(a, bcd, bcd.n)
                    a.nop(cycle_cnt=4000)
                    bcd.inc(a.dma_start(
                        out=sb_b[:, :],
                        in_=brow_d.ap()[r0 + 1:r0 + 2, :].broadcast_to((128, 512))))

                bcd_ln1 = bcd.n

                @block.tensor
                def _(t):
                    if th == 0:
                        _w(t, vx1, 32)
                    for k in range(16):
                        _w(t, aln, aln_b + min(k + 2, 16))
                        if k == 15:
                            t.nop(cycle_cnt=512)
                        t.matmul(ps1[0:1, :], lhsT=sones1[:, :],
                                 rhs=X1b[:, 512 * (k % 2):512 * (k % 2) + 512],
                                 start=(k == 0), stop=(k == 15))
                        tst.inc(t.matmul(pq1[0:1, :], lhsT=sones1[:, :],
                                         rhs=X1sq[:, 512 * (k % 2):512 * (k % 2) + 512],
                                         start=(k == 0), stop=(k == 15)))

                @block.vector
                def _(v):
                    _w(v, rws, rws_b + 1)
                    v.tensor_tensor(out=rowb[:, :], in0=rowm[:, :], in1=rowm[:, :],
                                    op=ALU.mult)
                    rws.inc(v.tensor_tensor(out=rowt[:, :], in0=rowv[:, :],
                                            in1=rowb[:, :], op=ALU.subtract))
                    _w(v, rws, rws_b + 3)
                    rws.inc(v.scalar_tensor_tensor(out=rowb[:, :], in0=rowm[:, :],
                                                   scalar=-1.0, in1=rowr[:, :],
                                                   op0=ALU.mult, op1=ALU.mult))
                    _w(v, bcd, bcd_ln1)
                    for k in range(16):
                        sx1 = X1[:, 1024 * k + 512 * th:1024 * k + 512 * th + 512]
                        v.tensor_tensor(out=t1[:, 512 * (k % 2):512 * (k % 2) + 512],
                                        in0=sx1, in1=sb_a[:, :], op=ALU.mult)
                        vap.inc(v.tensor_tensor(
                            out=X2b[:, 1024 * k + 512 * th:1024 * k + 512 * th + 512],
                            in0=t1[:, 512 * (k % 2):512 * (k % 2) + 512],
                            in1=sb_b[:, :], op=ALU.add))

        if "X2b" in tap_t:
            @block.scalar
            def _(a):
                _w(a, vap, 32)
                gtap.inc(a.dma_start(out=tap_t["X2b"], in_=X2b[:, :]))

        with (
            nc.psum_tensor("p_ps0", [2, 512], F32) as p_ps0,
            nc.psum_tensor("p_ps1", [2, 512], F32) as p_ps1,
            nc.psum_tensor("p_pb0", [128, 512], F32) as p_pb0,
            nc.psum_tensor("p_pb1", [128, 512], F32) as p_pb1,
        ):
            p_ps = [p_ps0, p_ps1]
            p_pb = [p_pb0, p_pb1]

            @block.tensor
            def _(t):
                if stop < 5:
                    return
                _w(t, vap, 32)
                for i, item in enumerate(p_items):
                    _w(t, pa, i + 1)
                    for ft in range(2):
                        fi = 2 * i + ft
                        _w(t, psc, fi - 1)
                        pm.inc(t.matmul(
                            p_ps[fi % 2][0:2, :], lhsT=sbones[:, :],
                            rhs=p_aw[:, 1024 * (i % 2) + 512 * ft:
                                     1024 * (i % 2) + 512 * ft + 512],
                            start=True, stop=True))
                    _w(t, pvr, i + 1)
                    for ft in range(2):
                        fi = 2 * i + ft
                        _w(t, pvb, fi - 1)
                        pb.inc(t.matmul(
                            p_pb[fi % 2][:, :], lhsT=sbcast2[:, :],
                            rhs=p_sr2[0:2, 1024 * (i % 2) + 512 * ft:
                                      1024 * (i % 2) + 512 * ft + 512],
                            start=True, stop=True))

            @block.scalar
            def _(a):
                if stop < 5:
                    return
                for i, item in enumerate(p_items):
                    _w(a, pw, i + 1)
                    sl_ = slice(1024 * (i % 2), 1024 * (i % 2) + 1024)
                    a.activation(out=p_aw[:, sl_], in_=p_wb[:, sl_], func=AF.Abs)
                    pa.inc(a.activation(out=p_sg[:, sl_], in_=p_wb[:, sl_],
                                        func=AF.Sign))
                    _w(a, pvr, i - 1)
                    for ft in range(2):
                        fi = 2 * i + ft
                        _w(a, pm, fi + 1)
                        psc.inc(a.activation(
                            out=p_sr[0:2, 1024 * (i % 2) + 512 * ft:
                                     1024 * (i % 2) + 512 * ft + 512],
                            in_=p_ps[fi % 2][0:2, :],
                            func=AF.Copy, scale=1.0 / 64.0))

            @block.vector
            def _(v):
                if stop < 5:
                    return
                for i, item in enumerate(p_items):
                    _w(v, psc, 2 * i + 2)
                    _w(v, pb, 2 * i - 2)
                    sl1 = slice(1024 * (i % 2), 1024 * (i % 2) + 1024)
                    pvr.inc(v.tensor_scalar_max(out=p_sr2[0:2, sl1],
                                                in0=p_sr[0:2, sl1], scalar1=1e-8))
                    for ft in range(2):
                        fi = 2 * i + ft
                        _w(v, pb, fi + 1)
                        if i >= 1:
                            _w(v, pqw, i - 1)
                        slf = slice(1024 * (i % 2) + 512 * ft,
                                    1024 * (i % 2) + 512 * ft + 512)
                        pbt = p_pb[fi % 2][:, :]
                        v.scalar_tensor_tensor(out=p_cmp[:, :], in0=pbt, scalar=QT,
                                               in1=p_aw[:, slf],
                                               op0=ALU.mult, op1=ALU.is_lt)
                        v.tensor_tensor(out=p_qs[:, :], in0=p_cmp[:, :],
                                        in1=p_sg[:, slf], op=ALU.mult)
                        pvb.inc(v.tensor_tensor(out=p_wf[:, slf], in0=p_qs[:, :],
                                                in1=pbt, op=ALU.mult))
                        v.tensor_tensor(out=p_cmp[:, :], in0=p_wf[:, slf],
                                        in1=p_wb[:, slf], op=ALU.subtract)
                        pvq.inc(v.tensor_tensor(out=p_wf[:, slf], in0=p_wb[:, slf],
                                                in1=p_cmp[:, :], op=ALU.add))

        for th in (range(2 if stop >= 7 else 1) if stop >= 6 else []):
            sl_b, tgu_b, asl_b, vh_b, ahb_b, ah2_b, tsq_b = \
                sl.n, tgu.n, asl.n, vh.n, ahb.n, ah2.n, tsq.n
            ghb_b = 64 * th
            shr_b, tpk_b, vx3_b, a2c_b = shr.n, tpk.n, vx3.n, a2c.n
            t2s_b, t2b_b, vo_b, so_b, arr_b = t2s.n, t2b.n, vo.n, so.n, arr_.n

            @block.sync
            def _(s):
                for oct_ in range(16):
                    for k in range(16):
                        ns = 16 * oct_ + k
                        if ns == 0:
                            _w(s, cc1, 1)
                            _w(s, cc2, 1)
                            _w(s, vap, 32)
                            if th == 1:
                                _w(s, so, so_b)
                        if oct_ >= 2:
                            _w(s, tgu, tgu_b + 4 * (oct_ - 1))
                        slot = ns % 32
                        ck, kk = k // 2, k % 2
                        sl.inc(s.dma_start(
                            out=sgu[:, 1024 * slot:1024 * (slot + 1)],
                            in_=gat_gu.ap()[4096 * ck + 256 * oct_ + 128 * kk:
                                            4096 * ck + 256 * oct_ + 128 * kk + 128, :]))
                for Dq in range(2):
                    for k in range(64):
                        nsl = 64 * Dq + k
                        if nsl == 0:
                            _w(s, cc2, 1)
                            _w(s, vap, 32)
                        if nsl >= 2:
                            _w(s, tpk, tpk_b + nsl - 1)
                        ck, kk = k // 8, k % 8
                        sl.inc(s.dma_start(
                            out=sprb[:, 1024 * (nsl % 2):1024 * (nsl % 2) + 1024],
                            in_=gat_pr.ap()[2048 * ck + 256 * kk + 128 * Dq:
                                            2048 * ck + 256 * kk + 128 * Dq + 128, :]))
                        _w(s, ghb, ghb_b + min(k + 3, 64))
                        if nsl >= 4:
                            _w(s, tpk, tpk_b + nsl - 3)
                        shr.inc(s.dma_start(
                            out=hbr[:, 512 * (nsl % 4):512 * (nsl % 4) + 512],
                            in_=hb_d.ap()[128 * k:128 * (k + 1), :]))

            with (
                nc.psum_tensor(f"pgt_{th}", [128, 1024], F32) as pgt,
                nc.psum_tensor(f"pup_{th}", [128, 1024], F32) as pup,
                nc.psum_tensor(f"pssq_{th}", [1, 512], F32) as pssq,
            ):
                @block.tensor
                def _(t):
                    for u in range(64):
                        oct_ = u // 4
                        _w(t, sl, sl_b + 16 * (oct_ + 1))
                        _w(t, vh, vh_b + u - 1)
                        if u == 0:
                            _w(t, vap, 32)
                            _w(t, pvb, 2 * NPI)
                            _w(t, psc, 2 * NPI)
                            if th == 1:
                                _w(t, vo, vo_b)
                                _w(t, arr_, arr_b)
                        for sg_ in range(2):
                            sig_l = (2 * u + sg_) % 8
                            dst = (pgt if sg_ == 0 else pup)[:, 512 * (u % 2):
                                                             512 * (u % 2) + 512]
                            for k in range(16):
                                slot = (16 * oct_ + k) % 32
                                inst = t.matmul(
                                    dst,
                                    lhsT=sgu[:, 1024 * slot + 128 * sig_l:
                                             1024 * slot + 128 * sig_l + 128],
                                    rhs=X2b[:, 1024 * k + 512 * th:
                                            1024 * k + 512 * th + 512],
                                    start=(k == 0), stop=(k == 15))
                        tgu.inc(inst)
                        if u >= 2:
                            _w(t, ah2, ah2_b + u - 1)
                            tsq.inc(t.matmul(
                                pssq[0:1, :], lhsT=sones1[:, :],
                                rhs=h2b[:, 512 * ((u - 2) % 4):
                                        512 * ((u - 2) % 4) + 512],
                                start=(u == 2), stop=False))
                    for uu in (62, 63):
                        _w(t, ah2, ah2_b + uu + 1)
                        tsq.inc(t.matmul(pssq[0:1, :], lhsT=sones1[:, :],
                                         rhs=h2b[:, 512 * (uu % 4):512 * (uu % 4) + 512],
                                         start=False, stop=(uu == 63)))
                tbc_prm = tbc.n

                @block.scalar
                def _(a):
                    for u in range(64):
                        _w(a, tgu, tgu_b + u + 1)
                        if u == 0:
                            _w(a, pvq, 2 * NPI)
                            _w(a, pqw, NPI)
                        _w(a, vh, vh_b + u - 1)
                        asl.inc(a.activation(
                            out=sg_t[:, 512 * (u % 2):512 * (u % 2) + 512],
                            in_=pgt[:, 512 * (u % 2):512 * (u % 2) + 512],
                            func=AF.Silu))
                        _w(a, vh, vh_b + u + 1)
                        _w(a, ghb, ghb_b + u - 1)
                        ahb.inc(a.activation(
                            out=hbt[:, 512 * (u % 2):512 * (u % 2) + 512],
                            in_=h_t[:, 512 * (u % 2):512 * (u % 2) + 512],
                            func=AF.Copy))
                        _w(a, tsq, tsq_b + u - 3)
                        ah2.inc(a.activation(
                            out=h2b[:, 512 * (u % 4):512 * (u % 4) + 512],
                            in_=h_t[:, 512 * (u % 2):512 * (u % 2) + 512],
                            func=AF.Square))
                    _w(a, tsq, tsq_b + 64)
                    a.nop(cycle_cnt=512)
                    a.activation(out=rowv[:, :], in_=pssq[0:1, :], func=AF.Ln,
                                 scale=1.0 / 8192.0, bias=sepsB[0:1, 0:1])
                    a.drain(fusable=False)
                    a.activation(out=rowr[:, :], in_=rowv[:, :],
                                 func=AF.Exp, scale=-0.5)
                    arr_.inc(a.drain(fusable=False))
                    if th == 0 and "rms0" in tap_t:
                        gtap.inc(a.dma_start(out=tap_t["rms0"][0:1, 1024:1536],
                                             in_=rowr[0:1, :]))
                    rr_ = 4 + th
                    if th == 1:
                        _w(a, vx3, 16)
                    bcd.inc(a.dma_start(out=brow_d.ap()[rr_:rr_ + 1, :],
                                        in_=rowr[0:1, :]))
                    _w(a, bcd, bcd.n)
                    a.nop(cycle_cnt=4000)
                    bcd.inc(a.dma_start(
                        out=rstd_sb[:, :],
                        in_=brow_d.ap()[rr_:rr_ + 1, :].broadcast_to((128, 512))))
                    if th == 0 and "rsb1" in tap_t:
                        _w(a, bcd, bcd.n)
                        gtap.inc(a.dma_start(out=tap_t["rsb1"], in_=rstd_sb[:, :]))
                    if th == 0 and "brow0" in tap_t:
                        _w(a, bcd, bcd.n)
                        gtap.inc(a.dma_start(out=tap_t["brow0"], in_=brow_d.ap()))

                bcd_rms = bcd.n

                @block.vector
                def _(v):
                    for u in range(64):
                        _w(v, asl, asl_b + u + 1)
                        _w(v, ah2, ah2_b + u - 1)
                        vh.inc(v.tensor_tensor(
                            out=h_t[:, 512 * (u % 2):512 * (u % 2) + 512],
                            in0=sg_t[:, 512 * (u % 2):512 * (u % 2) + 512],
                            in1=pup[:, 512 * (u % 2):512 * (u % 2) + 512],
                            op=ALU.mult))

            with nc.psum_tensor(f"ppr_{th}", [128, 8 * 512], F32) as ppr:
                @block.tensor
                def _(t):
                    for Dq in range(2):
                        for k in range(64):
                            nsl = 64 * Dq + k
                            _w(t, sl, sl_b + 256 + nsl + 1)
                            _w(t, shr, shr_b + nsl + 1)
                            if k == 0:
                                _w(t, vh, vh_b + 64)
                                _w(t, arr_, arr_b + 1)
                                _w(t, bcd, bcd_rms)
                                _w(t, vx3, vx3_b + 8 * Dq)
                            for jd in range(8):
                                inst = t.matmul(
                                    ppr[:, 512 * jd:512 * (jd + 1)],
                                    lhsT=sprb[:, 1024 * (nsl % 2) + 128 * jd:
                                              1024 * (nsl % 2) + 128 * jd + 128],
                                    rhs=hbr[:, 512 * (nsl % 4):512 * (nsl % 4) + 512],
                                    start=(k == 0), stop=(k == 63))
                            tpk.inc(inst)

                @block.vector
                def _(v):
                    for Dq in range(2):
                        _w(v, tpk, tpk_b + 64 * (Dq + 1))
                        for jd in range(8):
                            jo = 8 * Dq + jd
                            if th == 0 and jo == 0:
                                _w(v, tq3, 32)
                            v.tensor_tensor(out=t2[:, :],
                                            in0=ppr[:, 512 * jd:512 * (jd + 1)],
                                            in1=rstd_sb[:, :], op=ALU.mult)
                            vx3.inc(v.tensor_tensor(
                                out=X3q[:, 512 * jo:512 * (jo + 1)], in0=t2[:, :],
                                in1=X2b[:, 1024 * jo + 512 * th:
                                        1024 * jo + 512 * th + 512],
                                op=ALU.add))

            rws_b2 = rws.n
            with (
                nc.psum_tensor(f"ps2_{th}", [1, 512], F32) as ps2,
                nc.psum_tensor(f"pq2_{th}", [1, 512], F32) as pq2,
            ):
                @block.scalar
                def _(a):
                    if th == 0 and "X3q0" in tap_t:
                        _w(a, vx3, vx3_b + 16)
                        gtap.inc(a.dma_start(out=tap_t["X3q0"], in_=X3q[:, :]))
                    if th == 0 and "rsb0" in tap_t:
                        _w(a, vx3, vx3_b + 16)
                        gtap.inc(a.dma_start(out=tap_t["rsb0"], in_=rstd_sb[:, :]))
                    for jo in range(16):
                        if jo == 0:
                            _w(a, vx3, vx3_b + 16)
                        _w(a, t2s, t2s_b + jo)
                        src = X3q[:, 512 * jo:512 * (jo + 1)]
                        a.activation(out=X3b[:, :], in_=src, func=AF.Copy)
                        a2c.inc(a.activation(out=X3s[:, :], in_=src, func=AF.Square))
                    _w(a, t2s, t2s_b + 16)
                    a.nop(cycle_cnt=512)
                    a.activation(out=rowm[:, :], in_=ps2[0:1, :], func=AF.Copy,
                                 scale=1.0 / 2048.0)
                    a.activation(out=rowv[:, :], in_=pq2[0:1, :],
                                 func=AF.Copy, scale=1.0 / 2048.0)
                    rws.inc(a.drain(fusable=False))
                    _w(a, rws, rws_b2 + 2)
                    a.activation(out=rowv[:, :], in_=rowt[:, :], func=AF.Ln,
                                 bias=sepsA[0:1, 0:1])
                    a.drain(fusable=False)
                    a.activation(out=rowr[:, :], in_=rowv[:, :],
                                 func=AF.Exp, scale=-0.5)
                    rws.inc(a.drain(fusable=False))
                    r2 = 6 + 2 * th
                    _w(a, vap, 32)
                    if th == 1:
                        _w(a, vo, 16)
                    bcd.inc(a.dma_start(out=brow_d.ap()[r2:r2 + 1, :],
                                        in_=rowr[0:1, :]))
                    _w(a, bcd, bcd.n)
                    a.nop(cycle_cnt=4000)
                    bcd.inc(a.dma_start(
                        out=sb_a[:, :],
                        in_=brow_d.ap()[r2:r2 + 1, :].broadcast_to((128, 512))))
                    _w(a, rws, rws_b2 + 4)
                    bcd.inc(a.dma_start(out=brow_d.ap()[r2 + 1:r2 + 2, :],
                                        in_=rowb[0:1, :]))
                    _w(a, bcd, bcd.n)
                    a.nop(cycle_cnt=4000)
                    bcd.inc(a.dma_start(
                        out=sb_b[:, :],
                        in_=brow_d.ap()[r2 + 1:r2 + 2, :].broadcast_to((128, 512))))

                if th == 0 and "sba0" in tap_t:
                    @block.scalar
                    def _(a):
                        gtap.inc(a.dma_start(out=tap_t["sba0"], in_=sb_a[:, :]))
                        gtap.inc(a.dma_start(out=tap_t["sbb0"], in_=sb_b[:, :]))

                bcd_ln2 = bcd.n

                @block.tensor
                def _(t):
                    for jo in range(16):
                        _w(t, a2c, a2c_b + jo + 1)
                        t.nop(cycle_cnt=256)
                        t.matmul(ps2[0:1, :], lhsT=sones1[:, :], rhs=X3b[:, :],
                                 start=(jo == 0), stop=(jo == 15))
                        t2s.inc(t.matmul(pq2[0:1, :], lhsT=sones1[:, :],
                                         rhs=X3s[:, :],
                                         start=(jo == 0), stop=(jo == 15)))

                @block.vector
                def _(v):
                    _w(v, rws, rws_b2 + 1)
                    v.tensor_tensor(out=rowb[:, :], in0=rowm[:, :], in1=rowm[:, :],
                                    op=ALU.mult)
                    rws.inc(v.tensor_tensor(out=rowt[:, :], in0=rowv[:, :],
                                            in1=rowb[:, :], op=ALU.subtract))
                    _w(v, rws, rws_b2 + 3)
                    rws.inc(v.scalar_tensor_tensor(out=rowb[:, :], in0=rowm[:, :],
                                                   scalar=-1.0, in1=rowr[:, :],
                                                   op0=ALU.mult, op1=ALU.mult))
                    _w(v, bcd, bcd_ln2)
                    for jo in range(16):
                        _w(v, so, so_b + max(0, jo - 1))
                        v.tensor_tensor(out=t1[:, 0:512],
                                        in0=X3q[:, 512 * jo:512 * (jo + 1)],
                                        in1=sb_a[:, :], op=ALU.mult)
                        vo.inc(v.tensor_tensor(
                            out=outt[:, 512 * (jo % 2):512 * (jo % 2) + 512],
                            in0=t1[:, 0:512], in1=sb_b[:, :], op=ALU.add))

                @block.sync
                def _(s):
                    for jo in range(16):
                        _w(s, vo, vo_b + jo + 1)
                        so.inc(s.dma_start(
                            out=out[128 * jo:128 * (jo + 1), 512 * th:512 * th + 512],
                            in_=outt[:, 512 * (jo % 2):512 * (jo % 2) + 512]))

    return nc



def _host_inputs(L0_attn_x, L0_attn_wq, L0_attn_wk, L0_attn_wv, L0_attn_wo,
                 L0_ffn_w_gate_up, L0_ffn_w_proj):
    x = np.asarray(L0_attn_x)
    ws = {"wq": np.asarray(L0_attn_wq), "wk": np.asarray(L0_attn_wk),
          "wv": np.asarray(L0_attn_wv), "wo": np.asarray(L0_attn_wo)}
    wgu = np.asarray(L0_ffn_w_gate_up)
    wpr = np.asarray(L0_ffn_w_proj)

    warr = {}
    for nm, w in ws.items():
        t = w.reshape(16, 128, 16, 128)
        warr[nm] = np.ascontiguousarray(
            t.transpose(0, 3, 2, 1).reshape(16, 128, 2048))

    sig = np.arange(128)
    hperm = (((sig % 2) * 8192 + (sig // 2) * 128)[:, None]
             + np.arange(128)[None, :]).ravel()

    rr = np.arange(64)
    hh = np.arange(16)
    in_maps, metas = [], []
    for c in range(8):
        b, half = c // 2, c % 2
        s0 = 1024 * half
        i_rows = (rr[:, None] * 1 + 0)
        i_rows = (hh[None, :] * 128 + 64 * half + rr[:, None]).ravel()
        im = {
            "xT": np.ascontiguousarray(x[b, s0:s0 + 1024, :].T),
            "xresT": np.ascontiguousarray(x[b][i_rows, :].T),
            "wq": warr["wq"], "wk": warr["wk"], "wv": warr["wv"], "wo": warr["wo"],
            "wgu_sl": np.ascontiguousarray(wgu[hperm, 256 * c:256 * (c + 1)].T),
            "wpr_sl": np.ascontiguousarray(wpr[:, 1024 * c:1024 * (c + 1)].T),
        }
        in_maps.append(im)
        metas.append((b, i_rows))
    return in_maps, metas


_NC_CACHE = {}


def kernel(L0_attn_x, L0_attn_wq, L0_attn_wk, L0_attn_wv, L0_attn_wo,
           L0_ffn_w_gate_up, L0_ffn_w_proj):
    in_maps, metas = _host_inputs(L0_attn_x, L0_attn_wq, L0_attn_wk, L0_attn_wv,
                                  L0_attn_wo, L0_ffn_w_gate_up, L0_ffn_w_proj)
    if "nc" not in _NC_CACHE:
        _NC_CACHE["nc"] = build_program()
    nc = _NC_CACHE["nc"]
    res = run_bass_kernel_spmd(nc, in_maps, core_ids=list(range(8)))
    outp = np.empty((B, S, D), dtype=np.float32)
    for c in range(8):
        b, i_rows = metas[c]
        outp[b, i_rows, :] = res.results[c]["out"].T
    return outp


# version 03
# speedup vs baseline: 1.1004x; 1.1004x over previous
import contextlib

import numpy as np
import ml_dtypes
import concourse.bass as bass
import concourse.mybir as mybir
from concourse.bass_utils import run_bass_kernel_spmd

F32 = mybir.dt.float32
BF16 = mybir.dt.bfloat16
AF = mybir.ActivationFunctionType
ALU = mybir.AluOpType
AX = mybir.AxisListType
bf16 = ml_dtypes.bfloat16

B, S, D = 4, 2048, 2048
NG = 128
QT = float(np.float32(0.5009765625))
MASKVAL = -30000.0
SCALE_Q = float(np.float32(128.0 ** -0.5))
F32EPS = float(np.finfo(np.float32).eps)


class Chan:
    def __init__(self, ctx, nc, name, step=1):
        self.h = ctx.enter_context(nc.semaphore(name))
        self.step = step
        self.n = 0

    def inc(self, inst):
        inst.then_inc(self.h, self.step)
        self.n += 1
        return inst


def _w(eng, chan, n_events):
    if n_events > 0:
        eng.wait_ge(chan.h, n_events * chan.step)


def build_program(taps=(), stop=99):
    nc = bass.Bass("TRN2", target_bir_lowering=False, debug=False)

    xT = nc.dram_tensor("xT", [D, 1024], F32, kind="ExternalInput").ap()
    xresT = nc.dram_tensor("xresT", [D, 1024], F32, kind="ExternalInput").ap()
    wts = {nm: nc.dram_tensor(nm, [16, 128, 2048], F32, kind="ExternalInput").ap()
           for nm in ("wq", "wk", "wv", "wo")}
    wgu_sl = nc.dram_tensor("wgu_sl", [256, 16384], F32, kind="ExternalInput").ap()
    wpr_sl = nc.dram_tensor("wpr_sl", [1024, 2048], F32, kind="ExternalInput").ap()
    out = nc.dram_tensor("out", [D, 1024], F32, kind="ExternalOutput").ap()

    qsl_gu = nc.dram_tensor("qsl_gu", [4096, 1024], BF16)
    gat_gu = nc.dram_tensor("gat_gu", [8 * 4096, 1024], BF16)
    qsl_pr = nc.dram_tensor("qsl_pr", [2048, 1024], BF16)
    gat_pr = nc.dram_tensor("gat_pr", [8 * 2048, 1024], BF16)
    hb_d = nc.dram_tensor("hb_d", [8192, 512], BF16)
    brow_d = nc.dram_tensor("brow_d", [20, 512], F32)

    tap_specs = {"brow0": ([20, 512], F32), "rsb1": ([128, 512], F32),
                 "rows0": ([1, 4096], F32), "X3q0": ([128, 8192], F32),
                 "rms0": ([1, 2048], F32), "rsb0": ([128, 512], F32),
                 "sba0": ([128, 512], F32), "sbb0": ([128, 512], F32),
                 "Qbig": ([128, 16384], BF16), "Kbig": ([128, 16384], BF16),
                 "VOT": ([128, 16384], BF16), "X1": ([128, 16384], F32),
                 "X2b": ([128, 16384], BF16), "hbd": ([8192, 512], BF16),
                 "qgu": ([4096, 1024], BF16), "qpr": ([2048, 1024], BF16),
                 "pq0": ([128, 2048], F32)}
    tap_t = {t: nc.dram_tensor("tap_" + t, *tap_specs[t], kind="ExternalOutput").ap()
             for t in taps}

    p_i = np.arange(128)
    bones_np = np.stack([(p_i < 64), (p_i >= 64)], 1).astype(bf16)
    consts = {
        "ident": np.eye(128, dtype=bf16),
        "mask": np.where((p_i[:, None] % 8) == (p_i[None, :] % 8), 0.0,
                         MASKVAL).astype(np.float32),
        "ones1": np.ones((128, 1), dtype=bf16),
        "bones": bones_np,
        "bcast2": np.ascontiguousarray(bones_np.T),
    }
    cdr = {k: nc.inline_tensor(v, "c_" + k) for k, v in consts.items()}

    ctx = contextlib.ExitStack()
    with ctx:
        def SB(name, shape, dt):
            return ctx.enter_context(nc.sbuf_tensor(name, shape, dt))

        def SB_at(name, shape, dt, base, off_bytes=0):
            addr = nc.lookup_mloc(base).addr + off_bytes
            return nc.alloc_sbuf_tensor_at(name, shape, dt, offset=addr)

        sid = SB("sid", [128, 128], BF16)
        smask = SB("smask", [128, 128], F32)
        sones1 = SB("sones1", [128, 1], BF16)
        sbones = SB("sbones", [128, 2], BF16)
        sbcast2 = SB("sbcast2", [2, 128], BF16)
        sepsA = SB("sepsA", [1, 1], F32)
        sepsB = SB("sepsB", [1, 1], F32)
        rowm = SB("rowm", [1, 512], F32)
        rowv = SB("rowv", [1, 512], F32)
        rowr = SB("rowr", [1, 512], F32)
        rowb = SB("rowb", [1, 512], F32)
        rowt = SB("rowt", [1, 512], F32)
        wbuf = SB("wbuf", [128, 2 * 2048], BF16)
        X2b = SB("X2b", [128, 16 * 1024], BF16)
        xb = SB_at("xb", [128, 16 * 1024], BF16, X2b)

        p_wb = SB("p_wb", [128, 2 * 1024], BF16)
        p_aw = SB("p_aw", [128, 2 * 1024], BF16)
        p_sg = SB("p_sg", [128, 2 * 1024], BF16)
        p_wf = SB("p_wf", [128, 2 * 1024], BF16)
        p_sr = SB("p_sr", [2, 2 * 1024], BF16)
        p_sr2 = SB("p_sr2", [2, 2 * 1024], BF16)

        sg_t = SB_at("sg_t", [128, 2 * 512], F32, p_wb)
        h_t = SB_at("h_t", [128, 2 * 512], F32, p_aw)
        h2b = SB_at("h2b", [128, 4 * 512], BF16, p_sg)
        hbt = SB_at("hbt", [128, 2 * 512], BF16, p_wf)
        rstd_sb = SB_at("rstd_sb", [128, 512], F32, p_sr)

        Qbig = SB("Qbig", [128, 16 * 1024], BF16)
        Kbig = SB("Kbig", [128, 16 * 1024], BF16)
        X1 = SB_at("X1", [128, 16 * 1024], F32, Qbig)
        sgu = SB_at("sgu", [128, 32 * 1024], BF16, Qbig)

        VOT = SB("VOT", [128, 16 * 1024], BF16)
        X3q = SB_at("X3q", [128, 16 * 512], F32, VOT)

        Sm = SB("Sm", [128, 4 * 128], F32)
        Ebuf = SB("Ebuf", [128, 4 * 128], F32)
        Abuf = SB("Abuf", [128, 4 * 128], BF16)
        Ats = SB("Ats", [128, 4 * 128], BF16)
        Vss = SB("Vss", [128, 4 * 128], BF16)
        negmx = SB("negmx", [128, 4], F32)
        sumrow = SB("sumrow", [128, 4], F32)
        rcb = SB("rcb", [128, 4], F32)
        t2 = SB_at("t2", [128, 512], F32, Sm)
        qstg = SB_at("qstg", [128, 2 * 512], BF16, Sm)
        p_cmp = SB_at("p_cmp", [128, 512], BF16, Sm)
        p_qs = SB_at("p_qs", [128, 512], BF16, Ebuf)
        xres = SB("xres", [128, 3 * 1024], F32)
        sb_a = SB("sb_a", [128, 512], F32)
        sb_b = SB("sb_b", [128, 512], F32)
        X1b = SB("X1b", [128, 2 * 512], BF16)
        X1sq = SB("X1sq", [128, 2 * 512], BF16)
        t1 = SB("t1", [128, 2 * 512], F32)
        outt = SB("outt", [128, 2 * 512], F32)
        sprb = SB_at("sprb", [128, 2 * 1024], BF16, X1b)
        X3b = SB_at("X3b", [128, 512], BF16, t1)
        X3s = SB_at("X3s", [128, 512], BF16, t1, 1024)
        hbr = SB_at("hbr", [128, 4 * 512], BF16, t1)

        rem = nc.sbuf_bytes_remaining
        assert rem >= 0, f"SBUF over budget by {-rem} bytes"

        block = ctx.enter_context(nc.Block())
        CH = lambda name, step=1: Chan(ctx, nc, name, step)

        pw = CH("pw", 16); pa = CH("pa"); pm = CH("pm")
        psc = CH("psc"); pvr = CH("pvr"); pb = CH("pb"); pvq = CH("pvq")
        pvb = CH("pvb"); pqw = CH("pqw", 16); cc1 = CH("cc1"); cc2 = CH("cc2")
        gW = CH("gW", 16); tq = CH("tq"); aev = CH("aev"); vev = CH("vev")
        tS = CH("tS"); vme = CH("vme"); vmx = CH("vmx"); aex = CH("aex")
        vA = CH("vA"); tat = CH("tat"); vat = CH("vat"); tvs = CH("tvs")
        avs = CH("avs"); tm2 = CH("tm2"); aot = CH("aot")
        sx = CH("sx", 16); tq3 = CH("tq3"); vx1 = CH("vx1")
        aln = CH("aln"); tst = CH("tst"); rws = CH("rws"); tbc = CH("tbc")
        vap = CH("vap")
        sl = CH("sl", 16); tgu = CH("tgu"); asl = CH("asl"); vh = CH("vh")
        ahb = CH("ahb"); ghb = CH("ghb", 16); shr = CH("shr", 16)
        ah2 = CH("ah2"); tsq = CH("tsq"); arr_ = CH("arr_")
        tpk = CH("tpk"); vx3 = CH("vx3"); a2c = CH("a2c")
        t2s = CH("t2s"); t2b = CH("t2b"); vo = CH("vo"); so = CH("so", 16)
        bcd = CH("bcd", 16)
        cst = CH("cst", 16); gtap = CH("gtap", 16)

        p_items = [("gu", kk, sub) for kk in range(2) for sub in range(16)] + \
                  [("pr", kk, sub) for kk in range(8) for sub in range(2)]
        NPI = len(p_items)

        def p_src(item):
            w_, kk, sub = item
            t = wgu_sl if w_ == "gu" else wpr_sl
            return t[128 * kk:128 * (kk + 1), 1024 * sub:1024 * (sub + 1)]

        def p_dst(item):
            w_, kk, sub = item
            if w_ == "gu":
                return qsl_gu[256 * sub + 128 * kk:256 * sub + 128 * kk + 128, :]
            return qsl_pr[256 * kk + 128 * sub:256 * kk + 128 * sub + 128, :]

        @block.sync
        def _(s):
            for _i in range(8):
                s.nop(cycle_cnt=4000)
            for k, t in cdr.items():
                dst = {"ident": sid, "mask": smask, "ones1": sones1,
                       "bones": sbones, "bcast2": sbcast2}[k]
                cst.inc(s.dma_start(out=dst[:, :], in_=t.ap()))
            for jo in (range(16) if stop >= 3 else []):
                if jo >= 3:
                    _w(s, vx1, 2 * (jo - 2))
                sx.inc(s.dma_start(
                    out=xres[:, 1024 * (jo % 3):1024 * (jo % 3) + 1024],
                    in_=xresT[128 * jo:128 * (jo + 1), :]))

        @block.vector
        def _(v):
            for _i in range(8):
                v.nop(cycle_cnt=4000)
            v.memset(sepsA[:], 1e-5)
            v.memset(sepsB[:], F32EPS)

        @block.gpsimd
        def _(g):
            for _i in range(8):
                g.nop(cycle_cnt=4000)
            for k in range(16):
                gW.inc(g.dma_start(out=xb[:, 1024 * k:1024 * (k + 1)],
                                   in_=xT[128 * k:128 * (k + 1), :]))
            for pi, nm in enumerate(("wq", "wk", "wv")):
                for j in range(16):
                    L = pi * 16 + j
                    if L == 2:
                        _w(g, tq, 6)
                    elif L == 3:
                        _w(g, tq, 7)
                    elif L >= 4:
                        _w(g, tq, 2 * L)
                    gW.inc(g.dma_start(
                        out=wbuf[:, 2048 * (L % 2):2048 * (L % 2) + 2048],
                        in_=wts[nm][j]))
            for j in (range(16) if stop >= 3 else []):
                if j < 2:
                    _w(g, tq, 96 + 2 * j)
                else:
                    _w(g, tq3, 2 * (j - 2) + 2)
                gW.inc(g.dma_start(
                    out=wbuf[:, 2048 * (j % 2):2048 * (j % 2) + 2048],
                    in_=wts["wo"][j]))
            if stop >= 5:
                for i in (0, 1):
                    pw.inc(g.dma_start(
                        out=p_wb[:, 1024 * (i % 2):1024 * (i % 2) + 1024],
                        in_=p_src(p_items[i])))
                for i, item in enumerate(p_items):
                    _w(g, pvq, 2 * (i + 1))
                    pqw.inc(g.dma_start(
                        out=p_dst(item),
                        in_=p_wf[:, 1024 * (i % 2):1024 * (i % 2) + 1024]))
                    if i + 2 < len(p_items):
                        pw.inc(g.dma_start(
                            out=p_wb[:, 1024 * (i % 2):1024 * (i % 2) + 1024],
                            in_=p_src(p_items[i + 2])))
            if stop >= 5:
                _w(g, pqw, 48)
                cc1.inc(g.collective_compute(
                    "AllGather", ALU.bypass, replica_groups=[list(range(8))],
                    ins=[qsl_gu.ap().opt()], outs=[gat_gu.ap().opt()]))
                cc2.inc(g.collective_compute(
                    "AllGather", ALU.bypass, replica_groups=[list(range(8))],
                    ins=[qsl_pr.ap().opt()], outs=[gat_pr.ap().opt()]))
            for th in (range(2 if stop >= 7 else 1) if stop >= 6 else []):
                for u in range(64):
                    _w(g, ahb, 64 * th + u + 1)
                    ghb.inc(g.dma_start(
                        out=hb_d[128 * u:128 * (u + 1), :],
                        in_=hbt[:, 512 * (u % 2):512 * (u % 2) + 512]))
            if "qgu" in tap_t:
                _w(g, cc1, 1)
                gtap.inc(g.dma_start(out=tap_t["qgu"], in_=qsl_gu.ap()))
            if "qpr" in tap_t:
                _w(g, cc2, 1)
                gtap.inc(g.dma_start(out=tap_t["qpr"], in_=qsl_pr.ap()))
            if "hbd" in tap_t:
                _w(g, ghb, 128)
                gtap.inc(g.dma_start(out=tap_t["hbd"], in_=hb_d.ap()))
            if stop >= 7:
                g.wait_ge(so.h, 32 * 16)
            elif stop == 6:
                g.wait_ge(so.h, 16 * 16)
            elif stop == 5:
                g.wait_ge(cc2.h, 1)
                g.wait_ge(pqw.h, 48 * 16)
            elif stop == 4:
                g.wait_ge(vap.h, 32)
            elif stop == 3:
                g.wait_ge(vx1.h, 32)
            elif stop == 2:
                g.wait_ge(aot.h, 128)
            elif stop == 1:
                g.wait_ge(aev.h, 32)
                g.wait_ge(vev.h, 68)
            if taps:
                g.wait_ge(gtap.h, 16 * gtap.n)

        with nc.psum_tensor("pq", [128, 4 * 512], F32) as pq:
            @block.tensor
            def _(t):
                for _i in range(8):
                    t.nop(cycle_cnt=4000)
                _w(t, cst, 5)
                _w(t, gW, 16)
                sched = [(0, 0), (0, 1), (1, 0), (1, 1), (0, 0), (0, 1)]
                sched += [(L, th) for L in range(2, 48) for th in range(2)]
                for L, th in sched:
                    n = 2 * L + th
                    _w(t, gW, 16 + L + 1)
                    m4 = n - 4
                    if m4 >= 0:
                        if 32 <= m4 < 64:
                            _w(t, aev, m4 - 31)
                        elif m4 < 32:
                            _w(t, vev, 1 if m4 == 0 else
                               (3 if m4 == 1 else m4 + 3))
                        else:
                            _w(t, vev, m4 - 29)
                    for k in range(16):
                        inst = t.matmul(
                            pq[:, 512 * (n % 4):512 * (n % 4) + 512],
                            lhsT=wbuf[:, 2048 * (L % 2) + 128 * k:
                                      2048 * (L % 2) + 128 * (k + 1)],
                            rhs=xb[:, 1024 * k + 512 * th:
                                   1024 * k + 512 * th + 512],
                            start=(k == 0), stop=(k == 15))
                    tq.inc(inst)

            @block.scalar
            def _(a):
                for _i in range(8):
                    a.nop(cycle_cnt=4000)
                for j in range(16):
                    for th in range(2):
                        n = 2 * (16 + j) + th
                        _w(a, tq, min(n + 4, 98))
                        src = pq[:, 512 * (n % 4):512 * (n % 4) + 512]
                        aev.inc(a.activation(
                            out=Kbig[:, 1024 * j + 512 * th:
                                     1024 * j + 512 * th + 512],
                            in_=src, func=AF.Copy, scale=SCALE_Q))
                if "Qbig" in tap_t:
                    _w(a, vev, 68)
                    gtap.inc(a.dma_start(out=tap_t["Qbig"], in_=Qbig[:, :]))
                if "Kbig" in tap_t:
                    gtap.inc(a.dma_start(out=tap_t["Kbig"], in_=Kbig[:, :]))

            @block.vector
            def _(v):
                for _i in range(8):
                    v.memset(qstg[:, 512 * (_i % 2):512 * (_i % 2) + 512], 0.0)
                hq = Qbig[:, :].rearrange("p (g t) -> p g t", t=128)
                hv = VOT[:, :].rearrange("p (g t) -> p g t", t=128)
                for pi in (0, 2):
                    for j in range(16):
                        for th in range(2):
                            n = 2 * (pi * 16 + j) + th
                            S = 5 + n if n < 2 else n + 3
                            _w(v, tq, min(S + 1, 98))
                            if n >= 95:
                                v.nop(cycle_cnt=512)
                            src = pq[:, 512 * (n % 4):512 * (n % 4) + 512]
                            dst = (hq if pi == 0 else hv)[
                                :, 64 * th:64 * (th + 1), 8 * j:8 * (j + 1)]
                            if n < 2:
                                vev.inc(v.tensor_copy(
                                    out=qstg[:, 512 * th:512 * th + 512],
                                    in_=src))
                                vev.inc(v.tensor_copy(
                                    out=dst,
                                    in_=qstg[:, 512 * th:512 * th + 512]))
                            else:
                                vev.inc(v.tensor_copy(out=dst, in_=src))
                for th in range(2):
                    vev.inc(v.tensor_copy(
                        out=hq[:, 64 * th:64 * (th + 1), 0:8],
                        in_=qstg[:, 512 * th:512 * th + 512]))
                v.memset(sepsA[:], 1e-5)
                v.memset(sepsB[:], F32EPS)

        Kv3 = Kbig[:, :].rearrange("p (t s) -> p t s", t=16)

        def s2(g):
            return slice(512 * (g % 2), 512 * (g % 2) + 128)

        def s2w(g):
            return slice(1024 * (g % 2), 1024 * (g % 2) + 128)

        def b4(g):
            return slice(128 * (g % 4), 128 * (g % 4) + 128)

        with (
            nc.psum_tensor("pS", [128, 1024], F32) as pS,
            nc.psum_tensor("pat", [128, 2048], BF16) as pat,
            nc.psum_tensor("pvs", [128, 2048], BF16) as pvs,
            nc.psum_tensor("po2", [128, 1024], F32) as po2,
        ):
            @block.tensor
            def _(t):
                if stop < 2:
                    return
                _w(t, aev, 32)
                _w(t, vev, 68)
                for g in range(NG + 2):
                    if g >= 2:
                        x = g - 2
                        _w(t, vA, x + 1)
                        _w(t, vat, x - 1)
                        tat.inc(t.transpose(pat[:, s2w(x)], in_=Abuf[:, b4(x)],
                                            identity=sid[:, :]))
                    if g < NG:
                        _w(t, vme, g - 1)
                        tS.inc(t.matmul(pS[:, s2(g)],
                                        lhsT=Qbig[:, 128 * g:128 * (g + 1)],
                                        rhs=Kv3[:, :, 8 * g:8 * (g + 1)],
                                        start=True, stop=True))
                        _w(t, avs, g - 1)
                        tvs.inc(t.transpose(pvs[:, s2w(g)],
                                            in_=VOT[:, 128 * g:128 * (g + 1)],
                                            identity=sid[:, :]))
                    if g >= 2:
                        x = g - 2
                        _w(t, vat, x + 1)
                        _w(t, avs, x + 1)
                        _w(t, aot, x - 1)
                        tm2.inc(t.matmul(po2[:, s2(x)], lhsT=Vss[:, b4(x)],
                                         rhs=Ats[:, b4(x)], start=True, stop=True))

            @block.vector
            def _(v):
                if stop < 2:
                    return
                for g in range(NG + 2):
                    if g < NG:
                        _w(v, tS, g + 1)
                        _w(v, tvs, g + 1)
                        if g < 4:
                            v.nop(cycle_cnt=256)
                        _w(v, aex, (g - 4) // 2 + 1)
                        vme.inc(v.tensor_tensor(out=Sm[:, b4(g)], in0=pS[:, s2(g)],
                                                in1=smask[:, :], op=ALU.add))
                        vmx.inc(v.reduce_max(out=negmx[:, g % 4:g % 4 + 1],
                                             in_=Sm[:, b4(g)], axis=AX.X,
                                             negate=True))
                    if 1 <= g <= NG:
                        gA = g - 1
                        _w(v, aex, gA // 2 + 1)
                        _w(v, tat, gA - 3)
                        vA.inc(v.tensor_scalar_mul(out=Abuf[:, b4(gA)],
                                                   in0=Ebuf[:, b4(gA)],
                                                   scalar1=rcb[:, gA % 4:gA % 4 + 1]))
                    if g >= 2:
                        x = g - 2
                        _w(v, tat, x + 1)
                        _w(v, tS, min(x + 3, NG))
                        if x >= NG - 3:
                            v.nop(cycle_cnt=128)
                        _w(v, tm2, x - 3)
                        vat.inc(v.tensor_copy(out=Ats[:, b4(x)], in_=pat[:, s2w(x)]))

            @block.scalar
            def _(a):
                if stop < 2:
                    return
                for g in range(NG + 2):
                    if 1 <= g <= NG:
                        y = g - 1
                        _w(a, tvs, y + 1)
                        if y >= 2:
                            _w(a, tm2, y - 1)
                        else:
                            a.nop(cycle_cnt=512)
                        _w(a, tm2, y - 3)
                        avs.inc(a.activation(out=Vss[:, b4(y)], in_=pvs[:, s2w(y)],
                                             func=AF.Copy))
                    if g < NG:
                        _w(a, vmx, g + 1)
                        _w(a, vA, g - 3)
                        a.activation(out=Ebuf[:, b4(g)], in_=Sm[:, b4(g)],
                                     func=AF.Exp,
                                     bias=negmx[:, g % 4:g % 4 + 1],
                                     accum_out=sumrow[:, g % 4:g % 4 + 1])
                        if g % 2 == 1:
                            c0 = 2 * ((g % 4) // 2)
                            a.drain(fusable=False)
                            a.activation(out=negmx[:, c0:c0 + 2],
                                         in_=sumrow[:, c0:c0 + 2], func=AF.Ln)
                            a.drain(fusable=False)
                            a.activation(out=rcb[:, c0:c0 + 2],
                                         in_=negmx[:, c0:c0 + 2],
                                         func=AF.Exp, scale=-1.0)
                            aex.inc(a.drain(fusable=False))
                    if g >= 2:
                        x = g - 2
                        _w(a, tm2, x + 1)
                        _w(a, tat, min(x + 2, NG))
                        if x >= NG - 1:
                            a.nop(cycle_cnt=256)
                        dst = VOT[:, 128 * x:128 * (x + 1)].rearrange(
                            "p (s2 h2) -> p h2 s2", s2=8)
                        aot.inc(a.activation(out=dst, in_=po2[:, s2(x)],
                                             func=AF.Copy))
                if "VOT" in tap_t:
                    gtap.inc(a.dma_start(out=tap_t["VOT"], in_=VOT[:, :]))

        with nc.psum_tensor("po3", [128, 4 * 512], F32) as po3:
            @block.tensor
            def _(t):
                if stop < 3:
                    return
                _w(t, vme, NG); _w(t, vat, NG); _w(t, avs, NG); _w(t, aot, NG)
                for jo in range(16):
                    _w(t, gW, 16 + 48 + jo + 1)
                    for th in range(2):
                        n3 = 2 * jo + th
                        _w(t, vx1, n3 - 3)
                        v5 = VOT[:, 8192 * th:8192 * (th + 1)].rearrange(
                            "p (a b s h) -> p a b s h", a=32, b=2, s=8, h=16)
                        for m in range(16):
                            inst = t.matmul(
                                po3[:, 512 * (n3 % 4):512 * (n3 % 4) + 512],
                                lhsT=wbuf[:, 2048 * (jo % 2) + 128 * m:
                                          2048 * (jo % 2) + 128 * (m + 1)],
                                rhs=v5[:, :, m // 8, m % 8, :],
                                start=(m == 0), stop=(m == 15))
                        tq3.inc(inst)

            @block.vector
            def _(v):
                if stop < 3:
                    return
                for jo in range(16):
                    for th in range(2):
                        n3 = 2 * jo + th
                        _w(v, tq3, min(n3 + 2, 32))
                        if n3 >= 31:
                            v.nop(cycle_cnt=256)
                        _w(v, sx, jo + 1)
                        vx1.inc(v.tensor_tensor(
                            out=X1[:, 1024 * jo + 512 * th:1024 * jo + 512 * th + 512],
                            in0=po3[:, 512 * (n3 % 4):512 * (n3 % 4) + 512],
                            in1=xres[:, 1024 * (jo % 3) + 512 * th:
                                     1024 * (jo % 3) + 512 * th + 512],
                            op=ALU.add))

        for th in (range(2) if stop >= 4 else []):
            aln_b, tst_b, rws_b, tbc_b = aln.n, tst.n, rws.n, tbc.n
            with (
                nc.psum_tensor(f"ps1_{th}", [1, 512], F32) as ps1,
                nc.psum_tensor(f"pq1_{th}", [1, 512], F32) as pq1,
            ):
                @block.scalar
                def _(a):
                    if th == 0 and "X1" in tap_t:
                        _w(a, vx1, 32)
                        gtap.inc(a.dma_start(out=tap_t["X1"], in_=X1[:, :]))
                    for k in range(16):
                        if k == 0:
                            _w(a, vx1, 32)
                        _w(a, tst, tst_b + max(0, k - 1))
                        src = X1[:, 1024 * k + 512 * th:1024 * k + 512 * th + 512]
                        a.activation(out=X1b[:, 512 * (k % 2):512 * (k % 2) + 512],
                                     in_=src, func=AF.Copy)
                        aln.inc(a.activation(
                            out=X1sq[:, 512 * (k % 2):512 * (k % 2) + 512],
                            in_=src, func=AF.Square))
                    _w(a, tst, tst_b + 16)
                    a.nop(cycle_cnt=512)
                    a.activation(out=rowm[:, :], in_=ps1[0:1, :], func=AF.Copy,
                                 scale=1.0 / 2048.0)
                    a.activation(out=rowv[:, :], in_=pq1[0:1, :],
                                 func=AF.Copy, scale=1.0 / 2048.0)
                    rws.inc(a.drain(fusable=False))
                    _w(a, rws, rws_b + 2)
                    a.activation(out=rowv[:, :], in_=rowt[:, :], func=AF.Ln,
                                 bias=sepsA[0:1, 0:1])
                    a.drain(fusable=False)
                    a.activation(out=rowr[:, :], in_=rowv[:, :],
                                 func=AF.Exp, scale=-0.5)
                    rws.inc(a.drain(fusable=False))
                    if th == 1:
                        _w(a, vap, 16)
                    r0 = 2 * th
                    bcd.inc(a.dma_start(out=brow_d.ap()[r0:r0 + 1, :],
                                        in_=rowr[0:1, :]))
                    _w(a, bcd, bcd.n)
                    a.nop(cycle_cnt=4000)
                    bcd.inc(a.dma_start(
                        out=sb_a[:, :],
                        in_=brow_d.ap()[r0:r0 + 1, :].broadcast_to((128, 512))))
                    _w(a, rws, rws_b + 4)
                    bcd.inc(a.dma_start(out=brow_d.ap()[r0 + 1:r0 + 2, :],
                                        in_=rowb[0:1, :]))
                    _w(a, bcd, bcd.n)
                    a.nop(cycle_cnt=4000)
                    bcd.inc(a.dma_start(
                        out=sb_b[:, :],
                        in_=brow_d.ap()[r0 + 1:r0 + 2, :].broadcast_to((128, 512))))

                bcd_ln1 = bcd.n

                @block.tensor
                def _(t):
                    if th == 0:
                        _w(t, vx1, 32)
                    for k in range(16):
                        _w(t, aln, aln_b + min(k + 2, 16))
                        if k == 15:
                            t.nop(cycle_cnt=512)
                        t.matmul(ps1[0:1, :], lhsT=sones1[:, :],
                                 rhs=X1b[:, 512 * (k % 2):512 * (k % 2) + 512],
                                 start=(k == 0), stop=(k == 15))
                        tst.inc(t.matmul(pq1[0:1, :], lhsT=sones1[:, :],
                                         rhs=X1sq[:, 512 * (k % 2):512 * (k % 2) + 512],
                                         start=(k == 0), stop=(k == 15)))

                @block.vector
                def _(v):
                    _w(v, rws, rws_b + 1)
                    v.tensor_tensor(out=rowb[:, :], in0=rowm[:, :], in1=rowm[:, :],
                                    op=ALU.mult)
                    rws.inc(v.tensor_tensor(out=rowt[:, :], in0=rowv[:, :],
                                            in1=rowb[:, :], op=ALU.subtract))
                    _w(v, rws, rws_b + 3)
                    rws.inc(v.scalar_tensor_tensor(out=rowb[:, :], in0=rowm[:, :],
                                                   scalar=-1.0, in1=rowr[:, :],
                                                   op0=ALU.mult, op1=ALU.mult))
                    _w(v, bcd, bcd_ln1)
                    for k in range(16):
                        sx1 = X1[:, 1024 * k + 512 * th:1024 * k + 512 * th + 512]
                        v.tensor_tensor(out=t1[:, 512 * (k % 2):512 * (k % 2) + 512],
                                        in0=sx1, in1=sb_a[:, :], op=ALU.mult)
                        vap.inc(v.tensor_tensor(
                            out=X2b[:, 1024 * k + 512 * th:1024 * k + 512 * th + 512],
                            in0=t1[:, 512 * (k % 2):512 * (k % 2) + 512],
                            in1=sb_b[:, :], op=ALU.add))

        if "X2b" in tap_t:
            @block.scalar
            def _(a):
                _w(a, vap, 32)
                gtap.inc(a.dma_start(out=tap_t["X2b"], in_=X2b[:, :]))

        with (
            nc.psum_tensor("p_ps0", [2, 512], F32) as p_ps0,
            nc.psum_tensor("p_ps1", [2, 512], F32) as p_ps1,
            nc.psum_tensor("p_pb0", [128, 512], F32) as p_pb0,
            nc.psum_tensor("p_pb1", [128, 512], F32) as p_pb1,
        ):
            p_ps = [p_ps0, p_ps1]
            p_pb = [p_pb0, p_pb1]

            @block.tensor
            def _(t):
                if stop < 5:
                    return
                _w(t, vap, 32)
                for i, item in enumerate(p_items):
                    _w(t, pa, i + 1)
                    for ft in range(2):
                        fi = 2 * i + ft
                        _w(t, psc, fi - 1)
                        pm.inc(t.matmul(
                            p_ps[fi % 2][0:2, :], lhsT=sbones[:, :],
                            rhs=p_aw[:, 1024 * (i % 2) + 512 * ft:
                                     1024 * (i % 2) + 512 * ft + 512],
                            start=True, stop=True))
                    _w(t, pvr, i + 1)
                    for ft in range(2):
                        fi = 2 * i + ft
                        _w(t, pvb, fi - 1)
                        pb.inc(t.matmul(
                            p_pb[fi % 2][:, :], lhsT=sbcast2[:, :],
                            rhs=p_sr2[0:2, 1024 * (i % 2) + 512 * ft:
                                      1024 * (i % 2) + 512 * ft + 512],
                            start=True, stop=True))

            @block.scalar
            def _(a):
                if stop < 5:
                    return
                for i, item in enumerate(p_items):
                    _w(a, pw, i + 1)
                    sl_ = slice(1024 * (i % 2), 1024 * (i % 2) + 1024)
                    a.activation(out=p_aw[:, sl_], in_=p_wb[:, sl_], func=AF.Abs)
                    pa.inc(a.activation(out=p_sg[:, sl_], in_=p_wb[:, sl_],
                                        func=AF.Sign))
                    _w(a, pvr, i - 1)
                    for ft in range(2):
                        fi = 2 * i + ft
                        _w(a, pm, fi + 1)
                        psc.inc(a.activation(
                            out=p_sr[0:2, 1024 * (i % 2) + 512 * ft:
                                     1024 * (i % 2) + 512 * ft + 512],
                            in_=p_ps[fi % 2][0:2, :],
                            func=AF.Copy, scale=1.0 / 64.0))

            @block.vector
            def _(v):
                if stop < 5:
                    return
                for i, item in enumerate(p_items):
                    _w(v, psc, 2 * i + 2)
                    _w(v, pb, 2 * i - 2)
                    sl1 = slice(1024 * (i % 2), 1024 * (i % 2) + 1024)
                    pvr.inc(v.tensor_scalar_max(out=p_sr2[0:2, sl1],
                                                in0=p_sr[0:2, sl1], scalar1=1e-8))
                    for ft in range(2):
                        fi = 2 * i + ft
                        _w(v, pb, fi + 1)
                        if i >= 1:
                            _w(v, pqw, i - 1)
                        slf = slice(1024 * (i % 2) + 512 * ft,
                                    1024 * (i % 2) + 512 * ft + 512)
                        pbt = p_pb[fi % 2][:, :]
                        v.scalar_tensor_tensor(out=p_cmp[:, :], in0=pbt, scalar=QT,
                                               in1=p_aw[:, slf],
                                               op0=ALU.mult, op1=ALU.is_lt)
                        v.tensor_tensor(out=p_qs[:, :], in0=p_cmp[:, :],
                                        in1=p_sg[:, slf], op=ALU.mult)
                        pvb.inc(v.tensor_tensor(out=p_wf[:, slf], in0=p_qs[:, :],
                                                in1=pbt, op=ALU.mult))
                        v.tensor_tensor(out=p_cmp[:, :], in0=p_wf[:, slf],
                                        in1=p_wb[:, slf], op=ALU.subtract)
                        pvq.inc(v.tensor_tensor(out=p_wf[:, slf], in0=p_wb[:, slf],
                                                in1=p_cmp[:, :], op=ALU.add))

        for th in (range(2 if stop >= 7 else 1) if stop >= 6 else []):
            sl_b, tgu_b, asl_b, vh_b, ahb_b, ah2_b, tsq_b = \
                sl.n, tgu.n, asl.n, vh.n, ahb.n, ah2.n, tsq.n
            ghb_b = 64 * th
            shr_b, tpk_b, vx3_b, a2c_b = shr.n, tpk.n, vx3.n, a2c.n
            t2s_b, t2b_b, vo_b, so_b, arr_b = t2s.n, t2b.n, vo.n, so.n, arr_.n

            @block.sync
            def _(s):
                for oct_ in range(16):
                    for k in range(16):
                        ns = 16 * oct_ + k
                        if ns == 0:
                            _w(s, cc1, 1)
                            _w(s, cc2, 1)
                            _w(s, vap, 32)
                            if th == 1:
                                _w(s, so, so_b)
                        if oct_ >= 2:
                            _w(s, tgu, tgu_b + 4 * (oct_ - 1))
                        slot = ns % 32
                        ck, kk = k // 2, k % 2
                        sl.inc(s.dma_start(
                            out=sgu[:, 1024 * slot:1024 * (slot + 1)],
                            in_=gat_gu.ap()[4096 * ck + 256 * oct_ + 128 * kk:
                                            4096 * ck + 256 * oct_ + 128 * kk + 128, :]))
                for Dq in range(2):
                    for k in range(64):
                        nsl = 64 * Dq + k
                        if nsl == 0:
                            _w(s, cc2, 1)
                            _w(s, vap, 32)
                        if nsl >= 2:
                            _w(s, tpk, tpk_b + nsl - 1)
                        ck, kk = k // 8, k % 8
                        sl.inc(s.dma_start(
                            out=sprb[:, 1024 * (nsl % 2):1024 * (nsl % 2) + 1024],
                            in_=gat_pr.ap()[2048 * ck + 256 * kk + 128 * Dq:
                                            2048 * ck + 256 * kk + 128 * Dq + 128, :]))
                        _w(s, ghb, ghb_b + min(k + 3, 64))
                        if nsl >= 4:
                            _w(s, tpk, tpk_b + nsl - 3)
                        shr.inc(s.dma_start(
                            out=hbr[:, 512 * (nsl % 4):512 * (nsl % 4) + 512],
                            in_=hb_d.ap()[128 * k:128 * (k + 1), :]))

            with (
                nc.psum_tensor(f"pgt_{th}", [128, 1024], F32) as pgt,
                nc.psum_tensor(f"pup_{th}", [128, 1024], F32) as pup,
                nc.psum_tensor(f"pssq_{th}", [1, 512], F32) as pssq,
            ):
                @block.tensor
                def _(t):
                    for u in range(64):
                        oct_ = u // 4
                        _w(t, sl, sl_b + 16 * (oct_ + 1))
                        _w(t, vh, vh_b + u - 1)
                        if u == 0:
                            _w(t, vap, 32)
                            _w(t, pvb, 2 * NPI)
                            _w(t, psc, 2 * NPI)
                            if th == 1:
                                _w(t, vo, vo_b)
                                _w(t, arr_, arr_b)
                        for sg_ in range(2):
                            sig_l = (2 * u + sg_) % 8
                            dst = (pgt if sg_ == 0 else pup)[:, 512 * (u % 2):
                                                             512 * (u % 2) + 512]
                            for k in range(16):
                                slot = (16 * oct_ + k) % 32
                                inst = t.matmul(
                                    dst,
                                    lhsT=sgu[:, 1024 * slot + 128 * sig_l:
                                             1024 * slot + 128 * sig_l + 128],
                                    rhs=X2b[:, 1024 * k + 512 * th:
                                            1024 * k + 512 * th + 512],
                                    start=(k == 0), stop=(k == 15))
                        tgu.inc(inst)
                        if u >= 2:
                            _w(t, ah2, ah2_b + u - 1)
                            tsq.inc(t.matmul(
                                pssq[0:1, :], lhsT=sones1[:, :],
                                rhs=h2b[:, 512 * ((u - 2) % 4):
                                        512 * ((u - 2) % 4) + 512],
                                start=(u == 2), stop=False))
                    for uu in (62, 63):
                        _w(t, ah2, ah2_b + uu + 1)
                        tsq.inc(t.matmul(pssq[0:1, :], lhsT=sones1[:, :],
                                         rhs=h2b[:, 512 * (uu % 4):512 * (uu % 4) + 512],
                                         start=False, stop=(uu == 63)))
                tbc_prm = tbc.n

                @block.scalar
                def _(a):
                    for u in range(64):
                        _w(a, tgu, tgu_b + u + 1)
                        if u == 0:
                            _w(a, pvq, 2 * NPI)
                            _w(a, pqw, NPI)
                        _w(a, vh, vh_b + u - 1)
                        asl.inc(a.activation(
                            out=sg_t[:, 512 * (u % 2):512 * (u % 2) + 512],
                            in_=pgt[:, 512 * (u % 2):512 * (u % 2) + 512],
                            func=AF.Silu))
                        _w(a, vh, vh_b + u + 1)
                        _w(a, ghb, ghb_b + u - 1)
                        ahb.inc(a.activation(
                            out=hbt[:, 512 * (u % 2):512 * (u % 2) + 512],
                            in_=h_t[:, 512 * (u % 2):512 * (u % 2) + 512],
                            func=AF.Copy))
                        _w(a, tsq, tsq_b + u - 3)
                        ah2.inc(a.activation(
                            out=h2b[:, 512 * (u % 4):512 * (u % 4) + 512],
                            in_=h_t[:, 512 * (u % 2):512 * (u % 2) + 512],
                            func=AF.Square))
                    _w(a, tsq, tsq_b + 64)
                    a.nop(cycle_cnt=512)
                    a.activation(out=rowv[:, :], in_=pssq[0:1, :], func=AF.Ln,
                                 scale=1.0 / 8192.0, bias=sepsB[0:1, 0:1])
                    a.drain(fusable=False)
                    a.activation(out=rowr[:, :], in_=rowv[:, :],
                                 func=AF.Exp, scale=-0.5)
                    arr_.inc(a.drain(fusable=False))
                    if th == 0 and "rms0" in tap_t:
                        gtap.inc(a.dma_start(out=tap_t["rms0"][0:1, 1024:1536],
                                             in_=rowr[0:1, :]))
                    rr_ = 4 + th
                    if th == 1:
                        _w(a, vx3, 16)
                    bcd.inc(a.dma_start(out=brow_d.ap()[rr_:rr_ + 1, :],
                                        in_=rowr[0:1, :]))
                    _w(a, bcd, bcd.n)
                    a.nop(cycle_cnt=4000)
                    bcd.inc(a.dma_start(
                        out=rstd_sb[:, :],
                        in_=brow_d.ap()[rr_:rr_ + 1, :].broadcast_to((128, 512))))
                    if th == 0 and "rsb1" in tap_t:
                        _w(a, bcd, bcd.n)
                        gtap.inc(a.dma_start(out=tap_t["rsb1"], in_=rstd_sb[:, :]))
                    if th == 0 and "brow0" in tap_t:
                        _w(a, bcd, bcd.n)
                        gtap.inc(a.dma_start(out=tap_t["brow0"], in_=brow_d.ap()))

                bcd_rms = bcd.n

                @block.vector
                def _(v):
                    for u in range(64):
                        _w(v, asl, asl_b + u + 1)
                        _w(v, ah2, ah2_b + u - 1)
                        vh.inc(v.tensor_tensor(
                            out=h_t[:, 512 * (u % 2):512 * (u % 2) + 512],
                            in0=sg_t[:, 512 * (u % 2):512 * (u % 2) + 512],
                            in1=pup[:, 512 * (u % 2):512 * (u % 2) + 512],
                            op=ALU.mult))

            with nc.psum_tensor(f"ppr_{th}", [128, 8 * 512], F32) as ppr:
                @block.tensor
                def _(t):
                    for Dq in range(2):
                        for k in range(64):
                            nsl = 64 * Dq + k
                            _w(t, sl, sl_b + 256 + nsl + 1)
                            _w(t, shr, shr_b + nsl + 1)
                            if k == 0:
                                _w(t, vh, vh_b + 64)
                                _w(t, arr_, arr_b + 1)
                                _w(t, bcd, bcd_rms)
                                _w(t, vx3, vx3_b + 8 * Dq)
                            for jd in range(8):
                                inst = t.matmul(
                                    ppr[:, 512 * jd:512 * (jd + 1)],
                                    lhsT=sprb[:, 1024 * (nsl % 2) + 128 * jd:
                                              1024 * (nsl % 2) + 128 * jd + 128],
                                    rhs=hbr[:, 512 * (nsl % 4):512 * (nsl % 4) + 512],
                                    start=(k == 0), stop=(k == 63))
                            tpk.inc(inst)

                @block.vector
                def _(v):
                    for Dq in range(2):
                        _w(v, tpk, tpk_b + 64 * (Dq + 1))
                        for jd in range(8):
                            jo = 8 * Dq + jd
                            if th == 0 and jo == 0:
                                _w(v, tq3, 32)
                            v.tensor_tensor(out=t2[:, :],
                                            in0=ppr[:, 512 * jd:512 * (jd + 1)],
                                            in1=rstd_sb[:, :], op=ALU.mult)
                            vx3.inc(v.tensor_tensor(
                                out=X3q[:, 512 * jo:512 * (jo + 1)], in0=t2[:, :],
                                in1=X2b[:, 1024 * jo + 512 * th:
                                        1024 * jo + 512 * th + 512],
                                op=ALU.add))

            rws_b2 = rws.n
            with (
                nc.psum_tensor(f"ps2_{th}", [1, 512], F32) as ps2,
                nc.psum_tensor(f"pq2_{th}", [1, 512], F32) as pq2,
            ):
                @block.scalar
                def _(a):
                    if th == 0 and "X3q0" in tap_t:
                        _w(a, vx3, vx3_b + 16)
                        gtap.inc(a.dma_start(out=tap_t["X3q0"], in_=X3q[:, :]))
                    if th == 0 and "rsb0" in tap_t:
                        _w(a, vx3, vx3_b + 16)
                        gtap.inc(a.dma_start(out=tap_t["rsb0"], in_=rstd_sb[:, :]))
                    for jo in range(16):
                        if jo == 0:
                            _w(a, vx3, vx3_b + 16)
                        _w(a, t2s, t2s_b + jo)
                        src = X3q[:, 512 * jo:512 * (jo + 1)]
                        a.activation(out=X3b[:, :], in_=src, func=AF.Copy)
                        a2c.inc(a.activation(out=X3s[:, :], in_=src, func=AF.Square))
                    _w(a, t2s, t2s_b + 16)
                    a.nop(cycle_cnt=512)
                    a.activation(out=rowm[:, :], in_=ps2[0:1, :], func=AF.Copy,
                                 scale=1.0 / 2048.0)
                    a.activation(out=rowv[:, :], in_=pq2[0:1, :],
                                 func=AF.Copy, scale=1.0 / 2048.0)
                    rws.inc(a.drain(fusable=False))
                    _w(a, rws, rws_b2 + 2)
                    a.activation(out=rowv[:, :], in_=rowt[:, :], func=AF.Ln,
                                 bias=sepsA[0:1, 0:1])
                    a.drain(fusable=False)
                    a.activation(out=rowr[:, :], in_=rowv[:, :],
                                 func=AF.Exp, scale=-0.5)
                    rws.inc(a.drain(fusable=False))
                    r2 = 6 + 2 * th
                    _w(a, vap, 32)
                    if th == 1:
                        _w(a, vo, 16)
                    bcd.inc(a.dma_start(out=brow_d.ap()[r2:r2 + 1, :],
                                        in_=rowr[0:1, :]))
                    _w(a, bcd, bcd.n)
                    a.nop(cycle_cnt=4000)
                    bcd.inc(a.dma_start(
                        out=sb_a[:, :],
                        in_=brow_d.ap()[r2:r2 + 1, :].broadcast_to((128, 512))))
                    _w(a, rws, rws_b2 + 4)
                    bcd.inc(a.dma_start(out=brow_d.ap()[r2 + 1:r2 + 2, :],
                                        in_=rowb[0:1, :]))
                    _w(a, bcd, bcd.n)
                    a.nop(cycle_cnt=4000)
                    bcd.inc(a.dma_start(
                        out=sb_b[:, :],
                        in_=brow_d.ap()[r2 + 1:r2 + 2, :].broadcast_to((128, 512))))

                if th == 0 and "sba0" in tap_t:
                    @block.scalar
                    def _(a):
                        gtap.inc(a.dma_start(out=tap_t["sba0"], in_=sb_a[:, :]))
                        gtap.inc(a.dma_start(out=tap_t["sbb0"], in_=sb_b[:, :]))

                bcd_ln2 = bcd.n

                @block.tensor
                def _(t):
                    for jo in range(16):
                        _w(t, a2c, a2c_b + jo + 1)
                        t.nop(cycle_cnt=256)
                        t.matmul(ps2[0:1, :], lhsT=sones1[:, :], rhs=X3b[:, :],
                                 start=(jo == 0), stop=(jo == 15))
                        t2s.inc(t.matmul(pq2[0:1, :], lhsT=sones1[:, :],
                                         rhs=X3s[:, :],
                                         start=(jo == 0), stop=(jo == 15)))

                @block.vector
                def _(v):
                    _w(v, rws, rws_b2 + 1)
                    v.tensor_tensor(out=rowb[:, :], in0=rowm[:, :], in1=rowm[:, :],
                                    op=ALU.mult)
                    rws.inc(v.tensor_tensor(out=rowt[:, :], in0=rowv[:, :],
                                            in1=rowb[:, :], op=ALU.subtract))
                    _w(v, rws, rws_b2 + 3)
                    rws.inc(v.scalar_tensor_tensor(out=rowb[:, :], in0=rowm[:, :],
                                                   scalar=-1.0, in1=rowr[:, :],
                                                   op0=ALU.mult, op1=ALU.mult))
                    _w(v, bcd, bcd_ln2)
                    for jo in range(16):
                        _w(v, so, so_b + max(0, jo - 1))
                        v.tensor_tensor(out=t1[:, 0:512],
                                        in0=X3q[:, 512 * jo:512 * (jo + 1)],
                                        in1=sb_a[:, :], op=ALU.mult)
                        vo.inc(v.tensor_tensor(
                            out=outt[:, 512 * (jo % 2):512 * (jo % 2) + 512],
                            in0=t1[:, 0:512], in1=sb_b[:, :], op=ALU.add))

                @block.sync
                def _(s):
                    for jo in range(16):
                        _w(s, vo, vo_b + jo + 1)
                        so.inc(s.dma_start(
                            out=out[128 * jo:128 * (jo + 1), 512 * th:512 * th + 512],
                            in_=outt[:, 512 * (jo % 2):512 * (jo % 2) + 512]))

    return nc



def _host_inputs(L0_attn_x, L0_attn_wq, L0_attn_wk, L0_attn_wv, L0_attn_wo,
                 L0_ffn_w_gate_up, L0_ffn_w_proj):
    x = np.asarray(L0_attn_x)
    ws = {"wq": np.asarray(L0_attn_wq), "wk": np.asarray(L0_attn_wk),
          "wv": np.asarray(L0_attn_wv), "wo": np.asarray(L0_attn_wo)}
    wgu = np.asarray(L0_ffn_w_gate_up)
    wpr = np.asarray(L0_ffn_w_proj)

    warr = {}
    for nm, w in ws.items():
        t = w.reshape(16, 128, 16, 128)
        warr[nm] = np.ascontiguousarray(
            t.transpose(0, 3, 2, 1).reshape(16, 128, 2048))

    sig = np.arange(128)
    hperm = (((sig % 2) * 8192 + (sig // 2) * 128)[:, None]
             + np.arange(128)[None, :]).ravel()

    rr = np.arange(64)
    hh = np.arange(16)
    in_maps, metas = [], []
    for c in range(8):
        b, half = c // 2, c % 2
        s0 = 1024 * half
        i_rows = (rr[:, None] * 1 + 0)
        i_rows = (hh[None, :] * 128 + 64 * half + rr[:, None]).ravel()
        im = {
            "xT": np.ascontiguousarray(x[b, s0:s0 + 1024, :].T),
            "xresT": np.ascontiguousarray(x[b][i_rows, :].T),
            "wq": warr["wq"], "wk": warr["wk"], "wv": warr["wv"], "wo": warr["wo"],
            "wgu_sl": np.ascontiguousarray(wgu[hperm, 256 * c:256 * (c + 1)].T),
            "wpr_sl": np.ascontiguousarray(wpr[:, 1024 * c:1024 * (c + 1)].T),
        }
        in_maps.append(im)
        metas.append((b, i_rows))
    return in_maps, metas


_NC_CACHE = {}


def kernel(L0_attn_x, L0_attn_wq, L0_attn_wk, L0_attn_wv, L0_attn_wo,
           L0_ffn_w_gate_up, L0_ffn_w_proj):
    in_maps, metas = _host_inputs(L0_attn_x, L0_attn_wq, L0_attn_wk, L0_attn_wv,
                                  L0_attn_wo, L0_ffn_w_gate_up, L0_ffn_w_proj)
    if "nc" not in _NC_CACHE:
        _NC_CACHE["nc"] = build_program()
    nc = _NC_CACHE["nc"]
    res = run_bass_kernel_spmd(nc, in_maps, core_ids=list(range(8)))
    outp = np.empty((B, S, D), dtype=np.float32)
    for c in range(8):
        b, i_rows = metas[c]
        outp[b, i_rows, :] = res.results[c]["out"].T
    return outp


# version 04
# speedup vs baseline: 1.1409x; 1.0368x over previous
import contextlib

import numpy as np
import ml_dtypes
import concourse.bass as bass
import concourse.mybir as mybir
from concourse.bass_utils import run_bass_kernel_spmd

F32 = mybir.dt.float32
BF16 = mybir.dt.bfloat16
AF = mybir.ActivationFunctionType
ALU = mybir.AluOpType
AX = mybir.AxisListType
bf16 = ml_dtypes.bfloat16

B, S, D = 4, 2048, 2048
NG = 128
QT = float(np.float32(0.5009765625))
MASKVAL = -30000.0
SCALE_Q = float(np.float32(128.0 ** -0.5))
F32EPS = float(np.finfo(np.float32).eps)


class Chan:
    def __init__(self, ctx, nc, name, step=1):
        self.h = ctx.enter_context(nc.semaphore(name))
        self.step = step
        self.n = 0

    def inc(self, inst):
        inst.then_inc(self.h, self.step)
        self.n += 1
        return inst


def _w(eng, chan, n_events):
    if n_events > 0:
        eng.wait_ge(chan.h, n_events * chan.step)


def build_program(taps=(), stop=99):
    nc = bass.Bass("TRN2", target_bir_lowering=False, debug=False)

    xT = nc.dram_tensor("xT", [D, 1024], F32, kind="ExternalInput").ap()
    xresT = nc.dram_tensor("xresT", [D, 1024], F32, kind="ExternalInput").ap()
    wts = {nm: nc.dram_tensor(nm, [16, 128, 2048], F32, kind="ExternalInput").ap()
           for nm in ("wq", "wk", "wv", "wo")}
    wgu_sl = nc.dram_tensor("wgu_sl", [256, 16384], F32, kind="ExternalInput").ap()
    wpr_sl = nc.dram_tensor("wpr_sl", [1024, 2048], F32, kind="ExternalInput").ap()
    out = nc.dram_tensor("out", [D, 1024], F32, kind="ExternalOutput").ap()

    qsl_gu = nc.dram_tensor("qsl_gu", [4096, 1024], BF16)
    gat_gu = nc.dram_tensor("gat_gu", [8 * 4096, 1024], BF16)
    qsl_pr = nc.dram_tensor("qsl_pr", [2048, 1024], BF16)
    gat_pr = nc.dram_tensor("gat_pr", [8 * 2048, 1024], BF16)
    hb_d = nc.dram_tensor("hb_d", [8192, 512], BF16)
    brow_d = nc.dram_tensor("brow_d", [20, 512], F32)

    tap_specs = {"brow0": ([20, 512], F32), "rsb1": ([128, 512], F32),
                 "rows0": ([1, 4096], F32), "X3q0": ([128, 8192], F32),
                 "rms0": ([1, 2048], F32), "rsb0": ([128, 512], F32),
                 "sba0": ([128, 512], F32), "sbb0": ([128, 512], F32),
                 "Qbig": ([128, 16384], BF16), "Kbig": ([128, 16384], BF16),
                 "VOT": ([128, 16384], BF16), "X1": ([128, 16384], F32),
                 "X2b": ([128, 16384], BF16), "hbd": ([8192, 512], BF16),
                 "qgu": ([4096, 1024], BF16), "qpr": ([2048, 1024], BF16),
                 "pq0": ([128, 2048], F32)}
    tap_t = {t: nc.dram_tensor("tap_" + t, *tap_specs[t], kind="ExternalOutput").ap()
             for t in taps}

    p_i = np.arange(128)
    bones_np = np.stack([(p_i < 64), (p_i >= 64)], 1).astype(bf16)
    consts = {
        "ident": np.eye(128, dtype=bf16),
        "mask": np.where((p_i[:, None] % 8) == (p_i[None, :] % 8), 0.0,
                         MASKVAL).astype(np.float32),
        "ones1": np.ones((128, 1), dtype=bf16),
        "bones": bones_np,
        "bcast2": np.ascontiguousarray(bones_np.T),
    }
    cdr = {k: nc.inline_tensor(v, "c_" + k) for k, v in consts.items()}

    ctx = contextlib.ExitStack()
    with ctx:
        def SB(name, shape, dt):
            return ctx.enter_context(nc.sbuf_tensor(name, shape, dt))

        def SB_at(name, shape, dt, base, off_bytes=0):
            addr = nc.lookup_mloc(base).addr + off_bytes
            return nc.alloc_sbuf_tensor_at(name, shape, dt, offset=addr)

        sid = SB("sid", [128, 128], BF16)
        smask = SB("smask", [128, 128], F32)
        sones1 = SB("sones1", [128, 1], BF16)
        sbones = SB("sbones", [128, 2], BF16)
        sbcast2 = SB("sbcast2", [2, 128], BF16)
        sepsA = SB("sepsA", [1, 1], F32)
        sepsB = SB("sepsB", [1, 1], F32)
        rowm = SB("rowm", [1, 512], F32)
        rowv = SB("rowv", [1, 512], F32)
        rowr = SB("rowr", [1, 512], F32)
        rowb = SB("rowb", [1, 512], F32)
        rowt = SB("rowt", [1, 512], F32)
        wbuf = SB("wbuf", [128, 2 * 2048], BF16)
        X2b = SB("X2b", [128, 16 * 1024], BF16)
        xb = SB_at("xb", [128, 16 * 1024], BF16, X2b)

        p_wb = SB("p_wb", [128, 2 * 1024], BF16)
        p_aw = SB("p_aw", [128, 2 * 1024], BF16)
        p_sg = SB("p_sg", [128, 2 * 1024], BF16)
        p_wf = SB("p_wf", [128, 2 * 1024], BF16)
        p_sr = SB("p_sr", [2, 2 * 1024], BF16)
        p_sr2 = SB("p_sr2", [2, 2 * 1024], BF16)

        sg_t = SB_at("sg_t", [128, 2 * 512], F32, p_wb)
        h_t = SB_at("h_t", [128, 2 * 512], F32, p_aw)
        h2b = SB_at("h2b", [128, 4 * 512], BF16, p_sg)
        hbt = SB_at("hbt", [128, 2 * 512], BF16, p_wf)
        rstd_sb = SB_at("rstd_sb", [128, 512], F32, p_sr)

        Qbig = SB("Qbig", [128, 16 * 1024], BF16)
        Kbig = SB("Kbig", [128, 16 * 1024], BF16)
        X1 = SB_at("X1", [128, 16 * 1024], F32, Qbig)
        sgu = SB_at("sgu", [128, 32 * 1024], BF16, Qbig)

        VOT = SB("VOT", [128, 16 * 1024], BF16)
        X3q = SB_at("X3q", [128, 16 * 512], F32, VOT)

        Sm = SB("Sm", [128, 4 * 128], F32)
        Ebuf = SB("Ebuf", [128, 4 * 128], F32)
        Abuf = SB("Abuf", [128, 4 * 128], BF16)
        Ats = SB("Ats", [128, 4 * 128], BF16)
        Vss = SB("Vss", [128, 4 * 128], BF16)
        negmx = SB("negmx", [128, 4], F32)
        sumrow = SB("sumrow", [128, 4], F32)
        rcb = SB("rcb", [128, 4], F32)
        t2 = SB_at("t2", [128, 512], F32, Sm)
        qstg = SB_at("qstg", [128, 2 * 512], BF16, Sm)
        p_cmp = SB_at("p_cmp", [128, 512], BF16, Sm)
        p_qs = SB_at("p_qs", [128, 512], BF16, Ebuf)
        xres = SB("xres", [128, 3 * 1024], F32)
        sb_a = SB("sb_a", [128, 512], F32)
        sb_b = SB("sb_b", [128, 512], F32)
        X1b = SB("X1b", [128, 2 * 512], BF16)
        X1sq = SB("X1sq", [128, 2 * 512], BF16)
        t1 = SB("t1", [128, 2 * 512], F32)
        outt = SB("outt", [128, 2 * 512], F32)
        sprb = SB_at("sprb", [128, 2 * 1024], BF16, X1b)
        X3b = SB_at("X3b", [128, 512], BF16, t1)
        X3s = SB_at("X3s", [128, 512], BF16, t1, 1024)
        hbr = SB_at("hbr", [128, 4 * 512], BF16, t1)

        rem = nc.sbuf_bytes_remaining
        assert rem >= 0, f"SBUF over budget by {-rem} bytes"

        block = ctx.enter_context(nc.Block())
        CH = lambda name, step=1: Chan(ctx, nc, name, step)

        pw = CH("pw", 16); pa = CH("pa"); pm = CH("pm")
        psc = CH("psc"); pvr = CH("pvr"); pb = CH("pb"); pvq = CH("pvq")
        pvb = CH("pvb"); pqw = CH("pqw", 16); cc1 = CH("cc1"); cc2 = CH("cc2")
        gW = CH("gW", 16); tq = CH("tq"); aev = CH("aev"); vev = CH("vev")
        tS = CH("tS"); vme = CH("vme"); vmx = CH("vmx"); aex = CH("aex")
        vA = CH("vA"); tat = CH("tat"); vat = CH("vat"); tvs = CH("tvs")
        avs = CH("avs"); tm2 = CH("tm2"); aot = CH("aot")
        sx = CH("sx", 16); tq3 = CH("tq3"); vx1 = CH("vx1")
        aln = CH("aln"); tst = CH("tst"); rws = CH("rws"); tbc = CH("tbc")
        vap = CH("vap")
        sl = CH("sl", 16); tgu = CH("tgu"); asl = CH("asl"); vh = CH("vh")
        ahb = CH("ahb"); ghb = CH("ghb", 16); shr = CH("shr", 16)
        ah2 = CH("ah2"); tsq = CH("tsq"); arr_ = CH("arr_")
        tpk = CH("tpk"); vx3 = CH("vx3"); a2c = CH("a2c")
        t2s = CH("t2s"); t2b = CH("t2b"); vo = CH("vo"); so = CH("so", 16)
        bcd = CH("bcd", 16)
        cst = CH("cst", 16); gtap = CH("gtap", 16)

        p_items = [("gu", kk, sub) for kk in range(2) for sub in range(16)] + \
                  [("pr", kk, sub) for kk in range(8) for sub in range(2)]
        NPI = len(p_items)

        def p_src(item):
            w_, kk, sub = item
            t = wgu_sl if w_ == "gu" else wpr_sl
            return t[128 * kk:128 * (kk + 1), 1024 * sub:1024 * (sub + 1)]

        def p_dst(item):
            w_, kk, sub = item
            if w_ == "gu":
                return qsl_gu[256 * sub + 128 * kk:256 * sub + 128 * kk + 128, :]
            return qsl_pr[256 * kk + 128 * sub:256 * kk + 128 * sub + 128, :]

        @block.sync
        def _(s):
            for _i in range(8):
                s.nop(cycle_cnt=4000)
            for k, t in cdr.items():
                dst = {"ident": sid, "mask": smask, "ones1": sones1,
                       "bones": sbones, "bcast2": sbcast2}[k]
                cst.inc(s.dma_start(out=dst[:, :], in_=t.ap()))
            for jo in (range(16) if stop >= 3 else []):
                if jo >= 3:
                    _w(s, vx1, 2 * (jo - 2))
                sx.inc(s.dma_start(
                    out=xres[:, 1024 * (jo % 3):1024 * (jo % 3) + 1024],
                    in_=xresT[128 * jo:128 * (jo + 1), :]))

        @block.vector
        def _(v):
            for _i in range(8):
                v.nop(cycle_cnt=4000)
            v.memset(sepsA[:], 1e-5)
            v.memset(sepsB[:], F32EPS)

        @block.gpsimd
        def _(g):
            for _i in range(8):
                g.nop(cycle_cnt=4000)
            for k in range(16):
                gW.inc(g.dma_start(out=xb[:, 1024 * k:1024 * (k + 1)],
                                   in_=xT[128 * k:128 * (k + 1), :]))
            for pi, nm in enumerate(("wq", "wk", "wv")):
                for j in range(16):
                    L = pi * 16 + j
                    if L == 2:
                        _w(g, tq, 6)
                    elif L == 3:
                        _w(g, tq, 7)
                    elif L >= 4:
                        _w(g, tq, 2 * L)
                    gW.inc(g.dma_start(
                        out=wbuf[:, 2048 * (L % 2):2048 * (L % 2) + 2048],
                        in_=wts[nm][j]))
            for j in (range(16) if stop >= 3 else []):
                if j < 2:
                    _w(g, tq, 96 + 2 * j)
                else:
                    _w(g, tq3, 2 * (j - 2) + 2)
                gW.inc(g.dma_start(
                    out=wbuf[:, 2048 * (j % 2):2048 * (j % 2) + 2048],
                    in_=wts["wo"][j]))
            if stop >= 5:
                for i in (0, 1):
                    pw.inc(g.dma_start(
                        out=p_wb[:, 1024 * (i % 2):1024 * (i % 2) + 1024],
                        in_=p_src(p_items[i])))
                for i, item in enumerate(p_items):
                    _w(g, pvq, 2 * (i + 1))
                    pqw.inc(g.dma_start(
                        out=p_dst(item),
                        in_=p_wf[:, 1024 * (i % 2):1024 * (i % 2) + 1024]))
                    if i + 2 < len(p_items):
                        pw.inc(g.dma_start(
                            out=p_wb[:, 1024 * (i % 2):1024 * (i % 2) + 1024],
                            in_=p_src(p_items[i + 2])))
            if stop >= 5:
                _w(g, pqw, 48)
                cc1.inc(g.collective_compute(
                    "AllGather", ALU.bypass, replica_groups=[list(range(8))],
                    ins=[qsl_gu.ap().opt()], outs=[gat_gu.ap().opt()]))
                cc2.inc(g.collective_compute(
                    "AllGather", ALU.bypass, replica_groups=[list(range(8))],
                    ins=[qsl_pr.ap().opt()], outs=[gat_pr.ap().opt()]))
            for th in (range(2 if stop >= 7 else 1) if stop >= 6 else []):
                for u in range(64):
                    _w(g, ahb, 64 * th + u + 1)
                    ghb.inc(g.dma_start(
                        out=hb_d[128 * u:128 * (u + 1), :],
                        in_=hbt[:, 512 * (u % 2):512 * (u % 2) + 512]))
            if "qgu" in tap_t:
                _w(g, cc1, 1)
                gtap.inc(g.dma_start(out=tap_t["qgu"], in_=qsl_gu.ap()))
            if "qpr" in tap_t:
                _w(g, cc2, 1)
                gtap.inc(g.dma_start(out=tap_t["qpr"], in_=qsl_pr.ap()))
            if "hbd" in tap_t:
                _w(g, ghb, 128)
                gtap.inc(g.dma_start(out=tap_t["hbd"], in_=hb_d.ap()))
            if stop >= 7:
                g.wait_ge(so.h, 32 * 16)
            elif stop == 6:
                g.wait_ge(so.h, 16 * 16)
            elif stop == 5:
                g.wait_ge(cc2.h, 1)
                g.wait_ge(pqw.h, 48 * 16)
            elif stop == 4:
                g.wait_ge(vap.h, 32)
            elif stop == 3:
                g.wait_ge(vx1.h, 32)
            elif stop == 2:
                g.wait_ge(aot.h, 128)
            elif stop == 1:
                g.wait_ge(aev.h, 32)
                g.wait_ge(vev.h, 68)
            if taps:
                g.wait_ge(gtap.h, 16 * gtap.n)

        with nc.psum_tensor("pq", [128, 4 * 512], F32) as pq:
            @block.tensor
            def _(t):
                for _i in range(8):
                    t.nop(cycle_cnt=4000)
                _w(t, cst, 5)
                _w(t, gW, 16)
                sched = [(0, 0), (0, 1), (1, 0), (1, 1), (0, 0), (0, 1)]
                sched += [(L, th) for L in range(2, 48) for th in range(2)]
                for L, th in sched:
                    n = 2 * L + th
                    _w(t, gW, 16 + L + 1)
                    m4 = n - 4
                    if m4 >= 0:
                        if 32 <= m4 < 64:
                            _w(t, aev, m4 - 31)
                        elif m4 < 32:
                            _w(t, vev, 1 if m4 == 0 else
                               (3 if m4 == 1 else m4 + 3))
                        else:
                            _w(t, vev, m4 - 29)
                    for k in range(16):
                        inst = t.matmul(
                            pq[:, 512 * (n % 4):512 * (n % 4) + 512],
                            lhsT=wbuf[:, 2048 * (L % 2) + 128 * k:
                                      2048 * (L % 2) + 128 * (k + 1)],
                            rhs=xb[:, 1024 * k + 512 * th:
                                   1024 * k + 512 * th + 512],
                            start=(k == 0), stop=(k == 15))
                    tq.inc(inst)

            @block.scalar
            def _(a):
                for _i in range(8):
                    a.nop(cycle_cnt=4000)
                for j in range(16):
                    for th in range(2):
                        n = 2 * (16 + j) + th
                        _w(a, tq, min(n + 4, 98))
                        src = pq[:, 512 * (n % 4):512 * (n % 4) + 512]
                        aev.inc(a.activation(
                            out=Kbig[:, 1024 * j + 512 * th:
                                     1024 * j + 512 * th + 512],
                            in_=src, func=AF.Copy, scale=SCALE_Q))
                if "Qbig" in tap_t:
                    _w(a, vev, 68)
                    gtap.inc(a.dma_start(out=tap_t["Qbig"], in_=Qbig[:, :]))
                if "Kbig" in tap_t:
                    gtap.inc(a.dma_start(out=tap_t["Kbig"], in_=Kbig[:, :]))

            @block.vector
            def _(v):
                for _i in range(8):
                    v.memset(qstg[:, 512 * (_i % 2):512 * (_i % 2) + 512], 0.0)
                hq = Qbig[:, :].rearrange("p (g t) -> p g t", t=128)
                hv = VOT[:, :].rearrange("p (g t) -> p g t", t=128)
                for pi in (0, 2):
                    for j in range(16):
                        for th in range(2):
                            n = 2 * (pi * 16 + j) + th
                            S = 5 + n if n < 2 else n + 3
                            _w(v, tq, min(S + 1, 98))
                            if n >= 95:
                                v.nop(cycle_cnt=512)
                            src = pq[:, 512 * (n % 4):512 * (n % 4) + 512]
                            dst = (hq if pi == 0 else hv)[
                                :, 64 * th:64 * (th + 1), 8 * j:8 * (j + 1)]
                            if n < 2:
                                vev.inc(v.tensor_copy(
                                    out=qstg[:, 512 * th:512 * th + 512],
                                    in_=src))
                                vev.inc(v.tensor_copy(
                                    out=dst,
                                    in_=qstg[:, 512 * th:512 * th + 512]))
                            else:
                                vev.inc(v.tensor_copy(out=dst, in_=src))
                for th in range(2):
                    vev.inc(v.tensor_copy(
                        out=hq[:, 64 * th:64 * (th + 1), 0:8],
                        in_=qstg[:, 512 * th:512 * th + 512]))
                v.memset(sepsA[:], 1e-5)
                v.memset(sepsB[:], F32EPS)

        Kv3 = Kbig[:, :].rearrange("p (t s) -> p t s", t=16)

        def s2(g):
            return slice(512 * (g % 2), 512 * (g % 2) + 128)

        def s2w(g):
            return slice(1024 * (g % 2), 1024 * (g % 2) + 128)

        def b4(g):
            return slice(128 * (g % 4), 128 * (g % 4) + 128)

        with (
            nc.psum_tensor("pS", [128, 1024], F32) as pS,
            nc.psum_tensor("pat", [128, 2048], BF16) as pat,
            nc.psum_tensor("pvs", [128, 2048], BF16) as pvs,
            nc.psum_tensor("po2", [128, 1024], F32) as po2,
        ):
            @block.tensor
            def _(t):
                if stop < 2:
                    return
                _w(t, aev, 32)
                _w(t, vev, 68)
                for g in range(NG + 2):
                    if g >= 2:
                        x = g - 2
                        _w(t, vA, x + 1)
                        _w(t, vat, x - 1)
                        tat.inc(t.transpose(pat[:, s2w(x)], in_=Abuf[:, b4(x)],
                                            identity=sid[:, :]))
                    if g < NG:
                        _w(t, vme, g - 1)
                        tS.inc(t.matmul(pS[:, s2(g)],
                                        lhsT=Qbig[:, 128 * g:128 * (g + 1)],
                                        rhs=Kv3[:, :, 8 * g:8 * (g + 1)],
                                        start=True, stop=True))
                        _w(t, avs, g - 1)
                        tvs.inc(t.transpose(pvs[:, s2w(g)],
                                            in_=VOT[:, 128 * g:128 * (g + 1)],
                                            identity=sid[:, :]))
                    if g >= 2:
                        x = g - 2
                        _w(t, vat, x + 1)
                        _w(t, avs, x + 1)
                        _w(t, aot, x - 1)
                        tm2.inc(t.matmul(po2[:, s2(x)], lhsT=Vss[:, b4(x)],
                                         rhs=Ats[:, b4(x)], start=True, stop=True))

            @block.vector
            def _(v):
                if stop < 2:
                    return
                for g in range(NG + 2):
                    if g < NG:
                        _w(v, tS, g + 1)
                        _w(v, tvs, g + 1)
                        if g < 4:
                            v.nop(cycle_cnt=256)
                        _w(v, aex, (g - 4) // 2 + 1)
                        vme.inc(v.tensor_tensor(out=Sm[:, b4(g)], in0=pS[:, s2(g)],
                                                in1=smask[:, :], op=ALU.add))
                        vmx.inc(v.reduce_max(out=negmx[:, g % 4:g % 4 + 1],
                                             in_=Sm[:, b4(g)], axis=AX.X,
                                             negate=True))
                    if 1 <= g <= NG:
                        gA = g - 1
                        _w(v, aex, gA // 2 + 1)
                        _w(v, tat, gA - 3)
                        vA.inc(v.tensor_scalar_mul(out=Abuf[:, b4(gA)],
                                                   in0=Ebuf[:, b4(gA)],
                                                   scalar1=rcb[:, gA % 4:gA % 4 + 1]))
                    if g >= 2:
                        x = g - 2
                        _w(v, tat, x + 1)
                        _w(v, tS, min(x + 3, NG))
                        if x >= NG - 3:
                            v.nop(cycle_cnt=128)
                        _w(v, tm2, x - 3)
                        vat.inc(v.tensor_copy(out=Ats[:, b4(x)], in_=pat[:, s2w(x)]))

            @block.scalar
            def _(a):
                if stop < 2:
                    return
                for g in range(NG + 2):
                    if 1 <= g <= NG:
                        y = g - 1
                        _w(a, tvs, y + 1)
                        if y >= 2:
                            _w(a, tm2, y - 1)
                        else:
                            a.nop(cycle_cnt=512)
                        _w(a, tm2, y - 3)
                        avs.inc(a.activation(out=Vss[:, b4(y)], in_=pvs[:, s2w(y)],
                                             func=AF.Copy))
                    if g < NG:
                        _w(a, vmx, g + 1)
                        _w(a, vA, g - 3)
                        a.activation(out=Ebuf[:, b4(g)], in_=Sm[:, b4(g)],
                                     func=AF.Exp,
                                     bias=negmx[:, g % 4:g % 4 + 1],
                                     accum_out=sumrow[:, g % 4:g % 4 + 1])
                        if g % 2 == 1:
                            c0 = 2 * ((g % 4) // 2)
                            a.drain(fusable=False)
                            a.activation(out=negmx[:, c0:c0 + 2],
                                         in_=sumrow[:, c0:c0 + 2], func=AF.Ln)
                            a.drain(fusable=False)
                            a.activation(out=rcb[:, c0:c0 + 2],
                                         in_=negmx[:, c0:c0 + 2],
                                         func=AF.Exp, scale=-1.0)
                            aex.inc(a.drain(fusable=False))
                    if g >= 2:
                        x = g - 2
                        _w(a, tm2, x + 1)
                        _w(a, tat, min(x + 2, NG))
                        if x >= NG - 1:
                            a.nop(cycle_cnt=256)
                        dst = VOT[:, 128 * x:128 * (x + 1)].rearrange(
                            "p (s2 h2) -> p h2 s2", s2=8)
                        aot.inc(a.activation(out=dst, in_=po2[:, s2(x)],
                                             func=AF.Copy))
                if "VOT" in tap_t:
                    gtap.inc(a.dma_start(out=tap_t["VOT"], in_=VOT[:, :]))

        with nc.psum_tensor("po3", [128, 4 * 512], F32) as po3:
            @block.tensor
            def _(t):
                if stop < 3:
                    return
                _w(t, vme, NG); _w(t, vat, NG); _w(t, avs, NG); _w(t, aot, NG)
                for jo in range(16):
                    _w(t, gW, 16 + 48 + jo + 1)
                    for th in range(2):
                        n3 = 2 * jo + th
                        _w(t, vx1, n3 - 3)
                        v5 = VOT[:, 8192 * th:8192 * (th + 1)].rearrange(
                            "p (a b s h) -> p a b s h", a=32, b=2, s=8, h=16)
                        for m in range(16):
                            inst = t.matmul(
                                po3[:, 512 * (n3 % 4):512 * (n3 % 4) + 512],
                                lhsT=wbuf[:, 2048 * (jo % 2) + 128 * m:
                                          2048 * (jo % 2) + 128 * (m + 1)],
                                rhs=v5[:, :, m // 8, m % 8, :],
                                start=(m == 0), stop=(m == 15))
                        tq3.inc(inst)

            @block.vector
            def _(v):
                if stop < 3:
                    return
                for jo in range(16):
                    for th in range(2):
                        n3 = 2 * jo + th
                        _w(v, tq3, min(n3 + 2, 32))
                        if n3 >= 31:
                            v.nop(cycle_cnt=256)
                        _w(v, sx, jo + 1)
                        vx1.inc(v.tensor_tensor(
                            out=X1[:, 1024 * jo + 512 * th:1024 * jo + 512 * th + 512],
                            in0=po3[:, 512 * (n3 % 4):512 * (n3 % 4) + 512],
                            in1=xres[:, 1024 * (jo % 3) + 512 * th:
                                     1024 * (jo % 3) + 512 * th + 512],
                            op=ALU.add))

        with (
            nc.psum_tensor("p_ps0", [2, 512], F32) as p_ps0,
            nc.psum_tensor("p_ps1", [2, 512], F32) as p_ps1,
            nc.psum_tensor("p_pb0", [128, 512], F32) as p_pb0,
            nc.psum_tensor("p_pb1", [128, 512], F32) as p_pb1,
        ):
            p_ps = [p_ps0, p_ps1]
            p_pb = [p_pb0, p_pb1]

            @block.tensor
            def _(t):
                if stop < 5:
                    return
                _w(t, vx1, 32)
                for i, item in enumerate(p_items):
                    _w(t, pa, i + 1)
                    for ft in range(2):
                        fi = 2 * i + ft
                        _w(t, psc, fi - 1)
                        pm.inc(t.matmul(
                            p_ps[fi % 2][0:2, :], lhsT=sbones[:, :],
                            rhs=p_aw[:, 1024 * (i % 2) + 512 * ft:
                                     1024 * (i % 2) + 512 * ft + 512],
                            start=True, stop=True))
                    _w(t, pvr, i + 1)
                    for ft in range(2):
                        fi = 2 * i + ft
                        _w(t, pvb, fi - 1)
                        pb.inc(t.matmul(
                            p_pb[fi % 2][:, :], lhsT=sbcast2[:, :],
                            rhs=p_sr2[0:2, 1024 * (i % 2) + 512 * ft:
                                      1024 * (i % 2) + 512 * ft + 512],
                            start=True, stop=True))

            @block.scalar
            def _(a):
                if stop < 5:
                    return
                for i, item in enumerate(p_items):
                    _w(a, pw, i + 1)
                    sl_ = slice(1024 * (i % 2), 1024 * (i % 2) + 1024)
                    a.activation(out=p_aw[:, sl_], in_=p_wb[:, sl_], func=AF.Abs)
                    pa.inc(a.activation(out=p_sg[:, sl_], in_=p_wb[:, sl_],
                                        func=AF.Sign))
                    _w(a, pvr, i - 1)
                    for ft in range(2):
                        fi = 2 * i + ft
                        _w(a, pm, fi + 1)
                        psc.inc(a.activation(
                            out=p_sr[0:2, 1024 * (i % 2) + 512 * ft:
                                     1024 * (i % 2) + 512 * ft + 512],
                            in_=p_ps[fi % 2][0:2, :],
                            func=AF.Copy, scale=1.0 / 64.0))

            @block.vector
            def _(v):
                if stop < 5:
                    return
                for i, item in enumerate(p_items):
                    _w(v, psc, 2 * i + 2)
                    _w(v, pb, 2 * i - 2)
                    sl1 = slice(1024 * (i % 2), 1024 * (i % 2) + 1024)
                    pvr.inc(v.tensor_scalar_max(out=p_sr2[0:2, sl1],
                                                in0=p_sr[0:2, sl1], scalar1=1e-8))
                    for ft in range(2):
                        fi = 2 * i + ft
                        _w(v, pb, fi + 1)
                        if i >= 1:
                            _w(v, pqw, i - 1)
                        slf = slice(1024 * (i % 2) + 512 * ft,
                                    1024 * (i % 2) + 512 * ft + 512)
                        pbt = p_pb[fi % 2][:, :]
                        v.scalar_tensor_tensor(out=p_cmp[:, :], in0=pbt, scalar=QT,
                                               in1=p_aw[:, slf],
                                               op0=ALU.mult, op1=ALU.is_lt)
                        v.tensor_tensor(out=p_qs[:, :], in0=p_cmp[:, :],
                                        in1=p_sg[:, slf], op=ALU.mult)
                        pvb.inc(v.tensor_tensor(out=p_wf[:, slf], in0=p_qs[:, :],
                                                in1=pbt, op=ALU.mult))
                        v.tensor_tensor(out=p_cmp[:, :], in0=p_wf[:, slf],
                                        in1=p_wb[:, slf], op=ALU.subtract)
                        pvq.inc(v.tensor_tensor(out=p_wf[:, slf], in0=p_wb[:, slf],
                                                in1=p_cmp[:, :], op=ALU.add))

        for th in (range(2) if stop >= 4 else []):
            aln_b, tst_b, rws_b, tbc_b = aln.n, tst.n, rws.n, tbc.n
            with (
                nc.psum_tensor(f"ps1_{th}", [1, 512], F32) as ps1,
                nc.psum_tensor(f"pq1_{th}", [1, 512], F32) as pq1,
            ):
                @block.scalar
                def _(a):
                    if th == 0 and "X1" in tap_t:
                        _w(a, vx1, 32)
                        gtap.inc(a.dma_start(out=tap_t["X1"], in_=X1[:, :]))
                    for k in range(16):
                        if k == 0:
                            _w(a, vx1, 32)
                        _w(a, tst, tst_b + max(0, k - 1))
                        src = X1[:, 1024 * k + 512 * th:1024 * k + 512 * th + 512]
                        a.activation(out=X1b[:, 512 * (k % 2):512 * (k % 2) + 512],
                                     in_=src, func=AF.Copy)
                        aln.inc(a.activation(
                            out=X1sq[:, 512 * (k % 2):512 * (k % 2) + 512],
                            in_=src, func=AF.Square))
                    _w(a, tst, tst_b + 16)
                    a.nop(cycle_cnt=512)
                    a.activation(out=rowm[:, :], in_=ps1[0:1, :], func=AF.Copy,
                                 scale=1.0 / 2048.0)
                    a.activation(out=rowv[:, :], in_=pq1[0:1, :],
                                 func=AF.Copy, scale=1.0 / 2048.0)
                    rws.inc(a.drain(fusable=False))
                    _w(a, rws, rws_b + 2)
                    a.activation(out=rowv[:, :], in_=rowt[:, :], func=AF.Ln,
                                 bias=sepsA[0:1, 0:1])
                    a.drain(fusable=False)
                    a.activation(out=rowr[:, :], in_=rowv[:, :],
                                 func=AF.Exp, scale=-0.5)
                    rws.inc(a.drain(fusable=False))
                    if th == 1:
                        _w(a, vap, 16)
                    r0 = 2 * th
                    bcd.inc(a.dma_start(out=brow_d.ap()[r0:r0 + 1, :],
                                        in_=rowr[0:1, :]))
                    _w(a, bcd, bcd.n)
                    a.nop(cycle_cnt=4000)
                    bcd.inc(a.dma_start(
                        out=sb_a[:, :],
                        in_=brow_d.ap()[r0:r0 + 1, :].broadcast_to((128, 512))))
                    _w(a, rws, rws_b + 4)
                    bcd.inc(a.dma_start(out=brow_d.ap()[r0 + 1:r0 + 2, :],
                                        in_=rowb[0:1, :]))
                    _w(a, bcd, bcd.n)
                    a.nop(cycle_cnt=4000)
                    bcd.inc(a.dma_start(
                        out=sb_b[:, :],
                        in_=brow_d.ap()[r0 + 1:r0 + 2, :].broadcast_to((128, 512))))

                bcd_ln1 = bcd.n

                @block.tensor
                def _(t):
                    if th == 0:
                        _w(t, vx1, 32)
                    for k in range(16):
                        _w(t, aln, aln_b + min(k + 2, 16))
                        if k == 15:
                            t.nop(cycle_cnt=512)
                        t.matmul(ps1[0:1, :], lhsT=sones1[:, :],
                                 rhs=X1b[:, 512 * (k % 2):512 * (k % 2) + 512],
                                 start=(k == 0), stop=(k == 15))
                        tst.inc(t.matmul(pq1[0:1, :], lhsT=sones1[:, :],
                                         rhs=X1sq[:, 512 * (k % 2):512 * (k % 2) + 512],
                                         start=(k == 0), stop=(k == 15)))

                @block.vector
                def _(v):
                    _w(v, rws, rws_b + 1)
                    v.tensor_tensor(out=rowb[:, :], in0=rowm[:, :], in1=rowm[:, :],
                                    op=ALU.mult)
                    rws.inc(v.tensor_tensor(out=rowt[:, :], in0=rowv[:, :],
                                            in1=rowb[:, :], op=ALU.subtract))
                    _w(v, rws, rws_b + 3)
                    rws.inc(v.scalar_tensor_tensor(out=rowb[:, :], in0=rowm[:, :],
                                                   scalar=-1.0, in1=rowr[:, :],
                                                   op0=ALU.mult, op1=ALU.mult))
                    _w(v, bcd, bcd_ln1)
                    for k in range(16):
                        sx1 = X1[:, 1024 * k + 512 * th:1024 * k + 512 * th + 512]
                        v.tensor_tensor(out=t1[:, 512 * (k % 2):512 * (k % 2) + 512],
                                        in0=sx1, in1=sb_a[:, :], op=ALU.mult)
                        vap.inc(v.tensor_tensor(
                            out=X2b[:, 1024 * k + 512 * th:1024 * k + 512 * th + 512],
                            in0=t1[:, 512 * (k % 2):512 * (k % 2) + 512],
                            in1=sb_b[:, :], op=ALU.add))

        if "X2b" in tap_t:
            @block.scalar
            def _(a):
                _w(a, vap, 32)
                gtap.inc(a.dma_start(out=tap_t["X2b"], in_=X2b[:, :]))

        for th in (range(2 if stop >= 7 else 1) if stop >= 6 else []):
            sl_b, tgu_b, asl_b, vh_b, ahb_b, ah2_b, tsq_b = \
                sl.n, tgu.n, asl.n, vh.n, ahb.n, ah2.n, tsq.n
            ghb_b = 64 * th
            shr_b, tpk_b, vx3_b, a2c_b = shr.n, tpk.n, vx3.n, a2c.n
            t2s_b, t2b_b, vo_b, so_b, arr_b = t2s.n, t2b.n, vo.n, so.n, arr_.n

            @block.sync
            def _(s):
                for oct_ in range(16):
                    for k in range(16):
                        ns = 16 * oct_ + k
                        if ns == 0:
                            _w(s, cc1, 1)
                            _w(s, cc2, 1)
                            _w(s, vap, 32)
                            if th == 1:
                                _w(s, so, so_b)
                        if oct_ >= 2:
                            _w(s, tgu, tgu_b + 4 * (oct_ - 1))
                        slot = ns % 32
                        ck, kk = k // 2, k % 2
                        sl.inc(s.dma_start(
                            out=sgu[:, 1024 * slot:1024 * (slot + 1)],
                            in_=gat_gu.ap()[4096 * ck + 256 * oct_ + 128 * kk:
                                            4096 * ck + 256 * oct_ + 128 * kk + 128, :]))
                for Dq in range(2):
                    for k in range(64):
                        nsl = 64 * Dq + k
                        if nsl == 0:
                            _w(s, cc2, 1)
                            _w(s, vap, 32)
                        if nsl >= 2:
                            _w(s, tpk, tpk_b + nsl - 1)
                        ck, kk = k // 8, k % 8
                        sl.inc(s.dma_start(
                            out=sprb[:, 1024 * (nsl % 2):1024 * (nsl % 2) + 1024],
                            in_=gat_pr.ap()[2048 * ck + 256 * kk + 128 * Dq:
                                            2048 * ck + 256 * kk + 128 * Dq + 128, :]))
                        _w(s, ghb, ghb_b + min(k + 3, 64))
                        if nsl >= 4:
                            _w(s, tpk, tpk_b + nsl - 3)
                        shr.inc(s.dma_start(
                            out=hbr[:, 512 * (nsl % 4):512 * (nsl % 4) + 512],
                            in_=hb_d.ap()[128 * k:128 * (k + 1), :]))

            with (
                nc.psum_tensor(f"pgt_{th}", [128, 1024], F32) as pgt,
                nc.psum_tensor(f"pup_{th}", [128, 1024], F32) as pup,
                nc.psum_tensor(f"pssq_{th}", [1, 512], F32) as pssq,
            ):
                @block.tensor
                def _(t):
                    for u in range(64):
                        oct_ = u // 4
                        _w(t, sl, sl_b + 16 * (oct_ + 1))
                        _w(t, vh, vh_b + u - 1)
                        if u == 0:
                            _w(t, vap, 32)
                            _w(t, pvb, 2 * NPI)
                            _w(t, psc, 2 * NPI)
                            if th == 1:
                                _w(t, vo, vo_b)
                                _w(t, arr_, arr_b)
                        for sg_ in range(2):
                            sig_l = (2 * u + sg_) % 8
                            dst = (pgt if sg_ == 0 else pup)[:, 512 * (u % 2):
                                                             512 * (u % 2) + 512]
                            for k in range(16):
                                slot = (16 * oct_ + k) % 32
                                inst = t.matmul(
                                    dst,
                                    lhsT=sgu[:, 1024 * slot + 128 * sig_l:
                                             1024 * slot + 128 * sig_l + 128],
                                    rhs=X2b[:, 1024 * k + 512 * th:
                                            1024 * k + 512 * th + 512],
                                    start=(k == 0), stop=(k == 15))
                        tgu.inc(inst)
                        if u >= 2:
                            _w(t, ah2, ah2_b + u - 1)
                            tsq.inc(t.matmul(
                                pssq[0:1, :], lhsT=sones1[:, :],
                                rhs=h2b[:, 512 * ((u - 2) % 4):
                                        512 * ((u - 2) % 4) + 512],
                                start=(u == 2), stop=False))
                    for uu in (62, 63):
                        _w(t, ah2, ah2_b + uu + 1)
                        tsq.inc(t.matmul(pssq[0:1, :], lhsT=sones1[:, :],
                                         rhs=h2b[:, 512 * (uu % 4):512 * (uu % 4) + 512],
                                         start=False, stop=(uu == 63)))
                tbc_prm = tbc.n

                @block.scalar
                def _(a):
                    for u in range(64):
                        _w(a, tgu, tgu_b + u + 1)
                        if u == 0:
                            _w(a, pvq, 2 * NPI)
                            _w(a, pqw, NPI)
                        _w(a, vh, vh_b + u - 1)
                        asl.inc(a.activation(
                            out=sg_t[:, 512 * (u % 2):512 * (u % 2) + 512],
                            in_=pgt[:, 512 * (u % 2):512 * (u % 2) + 512],
                            func=AF.Silu))
                        _w(a, vh, vh_b + u + 1)
                        _w(a, ghb, ghb_b + u - 1)
                        ahb.inc(a.activation(
                            out=hbt[:, 512 * (u % 2):512 * (u % 2) + 512],
                            in_=h_t[:, 512 * (u % 2):512 * (u % 2) + 512],
                            func=AF.Copy))
                        _w(a, tsq, tsq_b + u - 3)
                        ah2.inc(a.activation(
                            out=h2b[:, 512 * (u % 4):512 * (u % 4) + 512],
                            in_=h_t[:, 512 * (u % 2):512 * (u % 2) + 512],
                            func=AF.Square))
                    _w(a, tsq, tsq_b + 64)
                    a.nop(cycle_cnt=512)
                    a.activation(out=rowv[:, :], in_=pssq[0:1, :], func=AF.Ln,
                                 scale=1.0 / 8192.0, bias=sepsB[0:1, 0:1])
                    a.drain(fusable=False)
                    a.activation(out=rowr[:, :], in_=rowv[:, :],
                                 func=AF.Exp, scale=-0.5)
                    arr_.inc(a.drain(fusable=False))
                    if th == 0 and "rms0" in tap_t:
                        gtap.inc(a.dma_start(out=tap_t["rms0"][0:1, 1024:1536],
                                             in_=rowr[0:1, :]))
                    rr_ = 4 + th
                    if th == 1:
                        _w(a, vx3, 16)
                    bcd.inc(a.dma_start(out=brow_d.ap()[rr_:rr_ + 1, :],
                                        in_=rowr[0:1, :]))
                    _w(a, bcd, bcd.n)
                    a.nop(cycle_cnt=4000)
                    bcd.inc(a.dma_start(
                        out=rstd_sb[:, :],
                        in_=brow_d.ap()[rr_:rr_ + 1, :].broadcast_to((128, 512))))
                    if th == 0 and "rsb1" in tap_t:
                        _w(a, bcd, bcd.n)
                        gtap.inc(a.dma_start(out=tap_t["rsb1"], in_=rstd_sb[:, :]))
                    if th == 0 and "brow0" in tap_t:
                        _w(a, bcd, bcd.n)
                        gtap.inc(a.dma_start(out=tap_t["brow0"], in_=brow_d.ap()))

                bcd_rms = bcd.n

                @block.vector
                def _(v):
                    for u in range(64):
                        _w(v, asl, asl_b + u + 1)
                        _w(v, ah2, ah2_b + u - 1)
                        vh.inc(v.tensor_tensor(
                            out=h_t[:, 512 * (u % 2):512 * (u % 2) + 512],
                            in0=sg_t[:, 512 * (u % 2):512 * (u % 2) + 512],
                            in1=pup[:, 512 * (u % 2):512 * (u % 2) + 512],
                            op=ALU.mult))

            with nc.psum_tensor(f"ppr_{th}", [128, 8 * 512], F32) as ppr:
                @block.tensor
                def _(t):
                    for Dq in range(2):
                        for k in range(64):
                            nsl = 64 * Dq + k
                            _w(t, sl, sl_b + 256 + nsl + 1)
                            _w(t, shr, shr_b + nsl + 1)
                            if k == 0:
                                _w(t, vh, vh_b + 64)
                                _w(t, arr_, arr_b + 1)
                                _w(t, bcd, bcd_rms)
                                _w(t, vx3, vx3_b + 8 * Dq)
                            for jd in range(8):
                                inst = t.matmul(
                                    ppr[:, 512 * jd:512 * (jd + 1)],
                                    lhsT=sprb[:, 1024 * (nsl % 2) + 128 * jd:
                                              1024 * (nsl % 2) + 128 * jd + 128],
                                    rhs=hbr[:, 512 * (nsl % 4):512 * (nsl % 4) + 512],
                                    start=(k == 0), stop=(k == 63))
                            tpk.inc(inst)

                @block.vector
                def _(v):
                    for Dq in range(2):
                        _w(v, tpk, tpk_b + 64 * (Dq + 1))
                        for jd in range(8):
                            jo = 8 * Dq + jd
                            if th == 0 and jo == 0:
                                _w(v, tq3, 32)
                            v.tensor_tensor(out=t2[:, :],
                                            in0=ppr[:, 512 * jd:512 * (jd + 1)],
                                            in1=rstd_sb[:, :], op=ALU.mult)
                            vx3.inc(v.tensor_tensor(
                                out=X3q[:, 512 * jo:512 * (jo + 1)], in0=t2[:, :],
                                in1=X2b[:, 1024 * jo + 512 * th:
                                        1024 * jo + 512 * th + 512],
                                op=ALU.add))

            rws_b2 = rws.n
            with (
                nc.psum_tensor(f"ps2_{th}", [1, 512], F32) as ps2,
                nc.psum_tensor(f"pq2_{th}", [1, 512], F32) as pq2,
            ):
                @block.scalar
                def _(a):
                    if th == 0 and "X3q0" in tap_t:
                        _w(a, vx3, vx3_b + 16)
                        gtap.inc(a.dma_start(out=tap_t["X3q0"], in_=X3q[:, :]))
                    if th == 0 and "rsb0" in tap_t:
                        _w(a, vx3, vx3_b + 16)
                        gtap.inc(a.dma_start(out=tap_t["rsb0"], in_=rstd_sb[:, :]))
                    for jo in range(16):
                        if jo == 0:
                            _w(a, vx3, vx3_b + 16)
                        _w(a, t2s, t2s_b + jo)
                        src = X3q[:, 512 * jo:512 * (jo + 1)]
                        a.activation(out=X3b[:, :], in_=src, func=AF.Copy)
                        a2c.inc(a.activation(out=X3s[:, :], in_=src, func=AF.Square))
                    _w(a, t2s, t2s_b + 16)
                    a.nop(cycle_cnt=512)
                    a.activation(out=rowm[:, :], in_=ps2[0:1, :], func=AF.Copy,
                                 scale=1.0 / 2048.0)
                    a.activation(out=rowv[:, :], in_=pq2[0:1, :],
                                 func=AF.Copy, scale=1.0 / 2048.0)
                    rws.inc(a.drain(fusable=False))
                    _w(a, rws, rws_b2 + 2)
                    a.activation(out=rowv[:, :], in_=rowt[:, :], func=AF.Ln,
                                 bias=sepsA[0:1, 0:1])
                    a.drain(fusable=False)
                    a.activation(out=rowr[:, :], in_=rowv[:, :],
                                 func=AF.Exp, scale=-0.5)
                    rws.inc(a.drain(fusable=False))
                    r2 = 6 + 2 * th
                    _w(a, vap, 32)
                    if th == 1:
                        _w(a, vo, 16)
                    bcd.inc(a.dma_start(out=brow_d.ap()[r2:r2 + 1, :],
                                        in_=rowr[0:1, :]))
                    _w(a, bcd, bcd.n)
                    a.nop(cycle_cnt=4000)
                    bcd.inc(a.dma_start(
                        out=sb_a[:, :],
                        in_=brow_d.ap()[r2:r2 + 1, :].broadcast_to((128, 512))))
                    _w(a, rws, rws_b2 + 4)
                    bcd.inc(a.dma_start(out=brow_d.ap()[r2 + 1:r2 + 2, :],
                                        in_=rowb[0:1, :]))
                    _w(a, bcd, bcd.n)
                    a.nop(cycle_cnt=4000)
                    bcd.inc(a.dma_start(
                        out=sb_b[:, :],
                        in_=brow_d.ap()[r2 + 1:r2 + 2, :].broadcast_to((128, 512))))

                if th == 0 and "sba0" in tap_t:
                    @block.scalar
                    def _(a):
                        gtap.inc(a.dma_start(out=tap_t["sba0"], in_=sb_a[:, :]))
                        gtap.inc(a.dma_start(out=tap_t["sbb0"], in_=sb_b[:, :]))

                bcd_ln2 = bcd.n

                @block.tensor
                def _(t):
                    for jo in range(16):
                        _w(t, a2c, a2c_b + jo + 1)
                        t.nop(cycle_cnt=256)
                        t.matmul(ps2[0:1, :], lhsT=sones1[:, :], rhs=X3b[:, :],
                                 start=(jo == 0), stop=(jo == 15))
                        t2s.inc(t.matmul(pq2[0:1, :], lhsT=sones1[:, :],
                                         rhs=X3s[:, :],
                                         start=(jo == 0), stop=(jo == 15)))

                @block.vector
                def _(v):
                    _w(v, rws, rws_b2 + 1)
                    v.tensor_tensor(out=rowb[:, :], in0=rowm[:, :], in1=rowm[:, :],
                                    op=ALU.mult)
                    rws.inc(v.tensor_tensor(out=rowt[:, :], in0=rowv[:, :],
                                            in1=rowb[:, :], op=ALU.subtract))
                    _w(v, rws, rws_b2 + 3)
                    rws.inc(v.scalar_tensor_tensor(out=rowb[:, :], in0=rowm[:, :],
                                                   scalar=-1.0, in1=rowr[:, :],
                                                   op0=ALU.mult, op1=ALU.mult))
                    _w(v, bcd, bcd_ln2)
                    for jo in range(16):
                        _w(v, so, so_b + max(0, jo - 1))
                        v.tensor_tensor(out=t1[:, 0:512],
                                        in0=X3q[:, 512 * jo:512 * (jo + 1)],
                                        in1=sb_a[:, :], op=ALU.mult)
                        vo.inc(v.tensor_tensor(
                            out=outt[:, 512 * (jo % 2):512 * (jo % 2) + 512],
                            in0=t1[:, 0:512], in1=sb_b[:, :], op=ALU.add))

                @block.sync
                def _(s):
                    for jo in range(16):
                        _w(s, vo, vo_b + jo + 1)
                        so.inc(s.dma_start(
                            out=out[128 * jo:128 * (jo + 1), 512 * th:512 * th + 512],
                            in_=outt[:, 512 * (jo % 2):512 * (jo % 2) + 512]))

    return nc



def _host_inputs(L0_attn_x, L0_attn_wq, L0_attn_wk, L0_attn_wv, L0_attn_wo,
                 L0_ffn_w_gate_up, L0_ffn_w_proj):
    x = np.asarray(L0_attn_x)
    ws = {"wq": np.asarray(L0_attn_wq), "wk": np.asarray(L0_attn_wk),
          "wv": np.asarray(L0_attn_wv), "wo": np.asarray(L0_attn_wo)}
    wgu = np.asarray(L0_ffn_w_gate_up)
    wpr = np.asarray(L0_ffn_w_proj)

    warr = {}
    for nm, w in ws.items():
        t = w.reshape(16, 128, 16, 128)
        warr[nm] = np.ascontiguousarray(
            t.transpose(0, 3, 2, 1).reshape(16, 128, 2048))

    sig = np.arange(128)
    hperm = (((sig % 2) * 8192 + (sig // 2) * 128)[:, None]
             + np.arange(128)[None, :]).ravel()

    rr = np.arange(64)
    hh = np.arange(16)
    in_maps, metas = [], []
    for c in range(8):
        b, half = c // 2, c % 2
        s0 = 1024 * half
        i_rows = (rr[:, None] * 1 + 0)
        i_rows = (hh[None, :] * 128 + 64 * half + rr[:, None]).ravel()
        im = {
            "xT": np.ascontiguousarray(x[b, s0:s0 + 1024, :].T),
            "xresT": np.ascontiguousarray(x[b][i_rows, :].T),
            "wq": warr["wq"], "wk": warr["wk"], "wv": warr["wv"], "wo": warr["wo"],
            "wgu_sl": np.ascontiguousarray(wgu[hperm, 256 * c:256 * (c + 1)].T),
            "wpr_sl": np.ascontiguousarray(wpr[:, 1024 * c:1024 * (c + 1)].T),
        }
        in_maps.append(im)
        metas.append((b, i_rows))
    return in_maps, metas


_NC_CACHE = {}


def kernel(L0_attn_x, L0_attn_wq, L0_attn_wk, L0_attn_wv, L0_attn_wo,
           L0_ffn_w_gate_up, L0_ffn_w_proj):
    in_maps, metas = _host_inputs(L0_attn_x, L0_attn_wq, L0_attn_wk, L0_attn_wv,
                                  L0_attn_wo, L0_ffn_w_gate_up, L0_ffn_w_proj)
    if "nc" not in _NC_CACHE:
        _NC_CACHE["nc"] = build_program()
    nc = _NC_CACHE["nc"]
    res = run_bass_kernel_spmd(nc, in_maps, core_ids=list(range(8)))
    outp = np.empty((B, S, D), dtype=np.float32)
    for c in range(8):
        b, i_rows = metas[c]
        outp[b, i_rows, :] = res.results[c]["out"].T
    return outp
